# version 45
# baseline (speedup 1.0000x reference)
import contextlib
import numpy as np
import ml_dtypes
import concourse.bass as bass
import concourse.mybir as mybir
from concourse.bass_utils import run_bass_kernel_spmd

F32 = mybir.dt.float32
BF16 = mybir.dt.bfloat16
I32 = mybir.dt.int32
AF = mybir.ActivationFunctionType
OP = mybir.AluOpType
AX = mybir.AxisListType
bfdt = ml_dtypes.bfloat16

ALPHA = 0.99
S = 8192
D = 1024
HID = 4096
NCC = 32
NDC = 8
CH = 512
NCHK = S // CH
TB = 512
NB = S // TB
NTL = S // 128
EPS = 1e-5
A512 = float(ALPHA ** CH)

ENGS = ("tensor", "vector", "scalar", "gpsimd", "sync")
NCONV_V = 22
NCONV_A = 24
NCONV_B = 20


class Ctr:
    __slots__ = ("h", "n")

    def __init__(self, h):
        self.h = h
        self.n = 0


class Rec:

    def __init__(self, chain=None):
        self.ops = {e: [] for e in ENGS}
        self.chain = chain or {}

    def emit(self, eng, fn, waits=(), inc=None, by=1):
        w = [(c.h, int(v)) for c, v in waits if v > 0]
        ch = self.chain.get(eng)
        if fn is not None and by != 16 and ch is not None:
            if ch.n > 0:
                w.append((ch.h, ch.n))
            assert inc is None or inc is ch
            inc = ch
            by = 1
        if inc is not None:
            inc.n += by
        self.ops[eng].append((tuple(w), fn, (inc.h if inc is not None else None),
                              by))
        return inc.n if inc is not None else None

    def replay(self, block):
        def mk(name):
            ops = self.ops[name]

            def run(e):
                for w, fn, inch, by in ops:
                    for h, v in w:
                        e.wait_ge(h, v)
                    ins = fn(e) if fn is not None else None
                    if inch is not None:
                        if ins is None:
                            ins = e.nop()
                        ins.then_inc(inch, by)
            return run
        block.tensor(mk("tensor"))
        block.vector(mk("vector"))
        block.scalar(mk("scalar"))
        block.gpsimd(mk("gpsimd"))
        block.sync(mk("sync"))


def build_program():
    nc = bass.Bass("TRN2", target_bir_lowering=False)

    x_h = nc.dram_tensor("x", [S, D], F32, kind="ExternalInput").ap()
    w1s_h = nc.dram_tensor("w1s", [NCC, 128, 1024], F32, kind="ExternalInput").ap()
    w2s_h = nc.dram_tensor("w2s", [NCC, 128, 1024], F32, kind="ExternalInput").ap()
    wqkvT_h = nc.dram_tensor("wqkvT", [D, 3 * D], F32, kind="ExternalInput").ap()
    bqkv_h = nc.dram_tensor("bqkvL", [128, 24], F32, kind="ExternalInput").ap()
    cw_h = nc.dram_tensor("cwL", [128, NCC, 3], F32, kind="ExternalInput").ap()
    cb_h = nc.dram_tensor("cbL", [128, NCC], F32, kind="ExternalInput").ap()
    identb_h = nc.dram_tensor("identb", [128, 128], BF16, kind="ExternalInput").ap()
    identf_h = nc.dram_tensor("identf", [128, 128], F32, kind="ExternalInput").ap()
    kcol_h = nc.dram_tensor("kcol", [128, 1], F32, kind="ExternalInput").ap()
    trik_h = nc.dram_tensor("trik", [128, 128], F32, kind="ExternalInput").ap()
    evk_h = nc.dram_tensor("evk", [128, 4], F32, kind="ExternalInput").ap()
    kdec_h = nc.dram_tensor("kdec", [128, 4], F32, kind="ExternalInput").ap()
    out_h = nc.dram_tensor("out", [S, D], F32, kind="ExternalOutput").ap()
    y1_h = nc.dram_tensor("y1scr", [S, D], F32, kind="Internal").ap()
    y2_h = nc.dram_tensor("y2scr", [S, D], F32, kind="Internal").ap()
    w1q_h = nc.dram_tensor("w1q", [NCC, 128, 1024], BF16, kind="Internal").ap()
    w2q_h = nc.dram_tensor("w2q", [NCC, 128, 1024], BF16, kind="Internal").ap()

    es = contextlib.ExitStack()

    def sb(name, shape, dt):
        return es.enter_context(nc.sbuf_tensor(name, shape, dt))

    def sem(name):
        return Ctr(es.enter_context(nc.semaphore(name)))

    identb = sb("identb_s", [128, 128], BF16)
    identf = sb("identf_s", [128, 128], F32)
    kcol = sb("kcol_s", [128, 1], F32)
    trik = sb("trik_s", [128, 128], F32)
    evk = sb("evk_s", [128, 4], F32)
    kdec = sb("kdec_s", [128, 4], F32)
    bqkv = sb("bqkv_s", [128, 24], F32)
    cwq = sb("cwq_s", [128, NCC, 3], F32)
    cwb = sb("cwb_s", [128, NCC, 3], BF16)
    cbv = sb("cbv_s", [128, NCC], F32)
    ones_c = sb("ones_c", [128, 1], F32)
    ones_r = sb("ones_r", [1, 128], F32)
    m1c = sb("m1c_s", [128, 1], F32)
    s1pp = sb("s1pp_s", [128, 1], F32)
    m2c = sb("m2c_s", [128, 1], F32)
    s2pp = sb("s2pp_s", [128, 1], F32)
    hs2 = sb("hs2_s", [128, 1], F32)
    stA_s = sb("stA_s", [128, NTL], F32)
    stA_q = sb("stA_q", [128, NTL], F32)
    mA = sb("mA", [128, NTL], F32)
    mrA = sb("mrA", [128, NTL], F32)
    rA = sb("rA", [128, NTL], F32)
    stB_s = sb("stB_s", [128, NTL], F32)
    stB_q = sb("stB_q", [128, NTL], F32)
    mB = sb("mB", [128, NTL], F32)
    rB = sb("rB", [128, NTL], F32)
    stC_s = sb("stC_s", [128, NTL], F32)
    stC_q = sb("stC_q", [128, NTL], F32)
    mC = sb("mC", [128, NTL], F32)
    mrC = sb("mrC", [128, NTL], F32)
    rC = sb("rC", [128, NTL], F32)
    sq_scr = sb("sq_scr", [128, 1024], F32)
    stt_tmp = sb("stt_tmp", [128, NTL], F32)

    cv = sem("cv")
    cs = sem("cs")
    cp = sem("cp")
    cg = sem("cg")
    bar = sem("bar")
    sxl = [sem(f"sxl{i}") for i in range(6)]
    sxr = [sem(f"sxr{i}") for i in range(4)]
    sst = [sem(f"sst{i}") for i in range(2)]
    sw1 = [sem(f"sw1{i}") for i in range(2)]
    sw2 = [sem(f"sw2{i}") for i in range(2)]
    syl = [sem(f"syl{i}") for i in range(8)]
    spr = [sem(f"spr{i}") for i in range(4)]
    spx = [sem(f"spx{i}") for i in range(4)]
    swq = sem("swq")
    sqs = [sem(f"sqs{i}") for i in range(2)]
    ld_small = sem("ld_small")

    def phase_barrier(rec):
        bar.n += len(ENGS)
        tgt = bar.n
        for e in ENGS:
            rec.ops[e].append(((), (lambda en: en.sem_inc(bar.h, 1)), None, 0))
            rec.ops[e].append((((bar.h, tgt),), None, None, 0))

    def rstd_batch(rec, sum_t, sq_t, m_t, r_t, mr_t=None):
        rec.emit("vector", lambda e: e.tensor_scalar(
            m_t[:], sum_t[:], 1.0 / D, None, OP.mult),
            waits=[(cs, cs.n)])
        rec.emit("vector", lambda e: e.tensor_tensor(
            stt_tmp[:], m_t[:], m_t[:], OP.mult))
        rec.emit("vector", lambda e: e.scalar_tensor_tensor(
            stt_tmp[:], sq_t[:], 1.0 / D, stt_tmp[:], OP.mult, OP.subtract))
        rec.emit("vector", lambda e: e.tensor_scalar(
            stt_tmp[:], stt_tmp[:], EPS, None, OP.add), inc=cv)
        rec.emit("scalar", lambda e: e.activation(stt_tmp[:], stt_tmp[:], AF.Ln),
                 waits=[(cv, cv.n)])
        rec.emit("scalar", lambda e: e.activation(
            r_t[:], stt_tmp[:], AF.Exp, scale=-0.5), inc=cs)
        if mr_t is not None:
            rec.emit("vector", lambda e: e.scalar_tensor_tensor(
                mr_t[:], m_t[:], -1.0, r_t[:], OP.mult, OP.mult),
                waits=[(cs, cs.n)])
        rec.emit("vector", lambda e: None, waits=[(cs, cs.n)], inc=cv)

    rec = Rec(chain={"vector": cv, "scalar": cs, "gpsimd": cg})
    with contextlib.ExitStack() as ps:
        wtile = [ps.enter_context(nc.sbuf_tensor(f"p_w{i}", [128, 1024], F32))
                 for i in range(4)]
        qtile = [ps.enter_context(nc.sbuf_tensor(f"p_q{i}", [128, 1024], BF16))
                 for i in range(2)]
        racc = ps.enter_context(nc.sbuf_tensor("p_racc", [128, 1], F32))
        rtmp = ps.enter_context(nc.sbuf_tensor("p_rtmp", [128, 1], F32))
        vtmp = ps.enter_context(nc.sbuf_tensor("p_vtmp", [128, 1024], F32))
        itmp = ps.enter_context(nc.sbuf_tensor("p_itmp", [128, 1024], I32))
        sc11 = ps.enter_context(nc.sbuf_tensor("p_sc11", [1, 1], F32))
        xt = [ps.enter_context(nc.sbuf_tensor(f"p_x{i}", [128, 1024], F32))
              for i in range(4)]
        paux = ps.enter_context(nc.psum_tensor("p_paux", [128, 512], F32))

        for dst, src in [(identb, identb_h), (identf, identf_h), (kcol, kcol_h),
                         (trik, trik_h), (evk, evk_h), (kdec, kdec_h),
                         (bqkv, bqkv_h), (cwq, cw_h), (cbv, cb_h)]:
            rec.emit("sync", lambda e, d=dst, s=src: e.dma_start(out=d[:], in_=s),
                     inc=ld_small, by=16)
        n_small = ld_small.n
        rec.emit("vector", lambda e: e.memset(ones_c[:], 1.0))
        rec.emit("vector", lambda e: e.memset(ones_r[:], 1.0), inc=cv)

        def prep_weights(ws_h, wq_dst, mc_dst, spp_dst):
            red_done = {}
            gate0 = cv.n
            for t in range(min(4, NCC)):
                rec.emit("sync", lambda e, t=t, d=wtile[t % 4]: e.dma_start(
                    out=d[:], in_=ws_h[t]), waits=[(cv, gate0)],
                    inc=spr[t % 4], by=16)
                ld_n[("p1", t)] = (spr[t % 4], spr[t % 4].n)
            rec.emit("vector", lambda e: e.memset(racc[:], 0.0), inc=cv)
            for t in range(NCC):
                slsem, slval = ld_n[("p1", t)]
                rec.emit("vector",
                         lambda e, d=wtile[t % 4]: e.tensor_reduce(
                             rtmp[:], d[:], AX.X, OP.add,
                             apply_absolute_value=True),
                         waits=[(slsem, slval)])
                rec.emit("vector",
                         lambda e: e.tensor_tensor(racc[:], rtmp[:], racc[:],
                                                   OP.add), inc=cv)
                red_done[t] = cv.n
                if t + 4 < NCC:
                    rec.emit("sync", lambda e, t2=t + 4, d=wtile[(t + 4) % 4]:
                             e.dma_start(out=d[:], in_=ws_h[t2]),
                             waits=[(cv, red_done[t])], inc=spr[t % 4], by=16)
                    ld_n[("p1", t + 4)] = (spr[t % 4], spr[t % 4].n)
            v_abs = cv.n
            rec.emit("tensor", lambda e: e.matmul(
                paux[0:1, 0:1], racc[:], ones_c[:], start=True, stop=True),
                waits=[(cv, v_abs)], inc=cp)
            rec.emit("vector", lambda e: e.tensor_scalar(
                sc11[:], paux[0:1, 0:1], 1.0 / (HID * D), EPS, OP.mult, OP.max),
                waits=[(cp, cp.n)], inc=cv)
            v_sc = cv.n
            rec.emit("tensor", lambda e: e.matmul(
                paux[:, 0:1], ones_r[:], sc11[:], start=True, stop=True),
                waits=[(cv, v_sc)], inc=cp)
            rec.emit("vector", lambda e, d=mc_dst: e.tensor_copy(d[:], paux[:, 0:1]),
                     waits=[(cp, cp.n)])
            rec.emit("vector", lambda e, d=spp_dst, m=mc_dst: e.reciprocal(
                d[:], m[:]), inc=cv)
            v_spp = cv.n
            q_done = {}
            for t in range(min(4, NCC)):
                rec.emit("sync", lambda e, t=t, d=wtile[t % 4]: e.dma_start(
                    out=d[:], in_=ws_h[t]), waits=[(cv, v_spp)],
                    inc=spr[t % 4], by=16)
                ld_n[("p2", t)] = (spr[t % 4], spr[t % 4].n)
            for t in range(NCC):
                slsem, slval = ld_n[("p2", t)]
                rec.emit("vector",
                         lambda e, d=wtile[t % 4], s=spp_dst: e.tensor_scalar(
                             vtmp[:], d[:], s[:, 0:1], None, OP.mult),
                         waits=[(slsem, slval)])
                rec.emit("vector", lambda e: e.tensor_copy(itmp[:], vtmp[:]))
                qd = qtile[t % 2]
                st_prev = sqs[t % 2].n
                rec.emit("vector", lambda e, q=qd: e.tensor_scalar(
                    q[:], itmp[:], -1.0, 1.0, OP.max, OP.min),
                    waits=[(sqs[t % 2], st_prev)], inc=cv)
                q_done[t] = cv.n
                rec.emit("gpsimd", lambda e, q=qd, t=t, d=wq_dst: e.dma_start(
                    out=d[t], in_=q[:]),
                    waits=[(cv, q_done[t])], inc=sqs[t % 2], by=16)
                if t + 4 < NCC:
                    rec.emit("sync", lambda e, t2=t + 4, d=wtile[(t + 4) % 4]:
                             e.dma_start(out=d[:], in_=ws_h[t2]),
                             waits=[(cv, q_done[t])], inc=spr[t % 4], by=16)
                    ld_n[("p2", t + 4)] = (spr[t % 4], spr[t % 4].n)
            return

        ld_n = {}
        prep_weights(w1s_h, w1q_h, m1c, s1pp)
        v_w1done = cv.n
        prep_weights(w2s_h, w2q_h, m2c, s2pp)

        rec.emit("vector", lambda e: e.tensor_scalar(
            cwq[:].rearrange("p a b -> p (a b)"),
            cwq[:].rearrange("p a b -> p (a b)"), m1c[:, 0:1], None, OP.mult),
            waits=[(ld_small, n_small), (cv, v_w1done)])
        rec.emit("vector", lambda e: e.tensor_copy(
            cwb[:].rearrange("p a b -> p (a b)"),
            cwq[:].rearrange("p a b -> p (a b)")))
        rec.emit("vector", lambda e: e.tensor_scalar(
            hs2[:], m2c[:], 0.5 / 127.0, None, OP.mult), inc=cv)

        stx_done = {}
        for k in range(min(4, NTL)):
            rec.emit("gpsimd", lambda e, k=k, d=xt[k % 4]: e.dma_start(
                out=d[:], in_=x_h[k * 128:(k + 1) * 128, :]),
                inc=spx[k % 4], by=16)
            ld_n[("x", k)] = (spx[k % 4], spx[k % 4].n)
        for k in range(NTL):
            slsem, slval = ld_n[("x", k)]
            rec.emit("vector", lambda e, k=k, d=xt[k % 4]: e.tensor_reduce(
                stA_s[:, k:k + 1], d[:], AX.X, OP.add),
                waits=[(slsem, slval)])
            rec.emit("scalar", lambda e, k=k, d=xt[k % 4]: e.activation(
                sq_scr[:], d[:], AF.Square, accum_out=stA_q[:, k:k + 1]),
                waits=[(slsem, slval)], inc=cs)
            rec.emit("vector", lambda e: None, waits=[(cs, cs.n)], inc=cv)
            stx_done[k] = cv.n
            if k + 4 < NTL:
                rec.emit("gpsimd", lambda e, k2=k + 4, d=xt[(k + 4) % 4]:
                         e.dma_start(out=d[:], in_=x_h[k2 * 128:(k2 + 1) * 128, :]),
                         waits=[(cv, stx_done[k])], inc=spx[k % 4], by=16)
                ld_n[("x", k + 4)] = (spx[k % 4], spx[k % 4].n)

        rstd_batch(rec, stA_s, stA_q, mA, rA, mrA)
        rec.emit("sync", None, waits=[(ld_small, n_small)])
        for ssem in sqs:
            rec.emit("sync", None, waits=[(ssem, ssem.n)])
        phase_barrier(rec)
        with nc.Block("prep") as blk:
            rec.replay(blk)

    def ffn_phase(src_h, dst_h, m_t, r_t, mr_t, out_sum, out_sq, name):
        pfx = name + "_"
        rec = Rec(chain={"vector": cv, "scalar": cs, "gpsimd": cg})
        with contextlib.ExitStack() as ps:
            w1r = ps.enter_context(nc.sbuf_tensor(pfx + "w1r", [128, 8, 8, 128], BF16))
            w2r = ps.enter_context(nc.sbuf_tensor(pfx + "w2r", [128, 8, 1024], BF16))
            xln = [ps.enter_context(nc.sbuf_tensor(pfx + f"xln{i}", [128, 1024], F32))
                   for i in range(3)]
            xres = [ps.enter_context(nc.sbuf_tensor(pfx + f"xres{i}", [128, 1024], F32))
                    for i in range(2)]
            m1tm = [ps.enter_context(nc.sbuf_tensor(pfx + f"m1tm{i}", [128, 1024], BF16))
                    for i in range(2)]
            m1f = ps.enter_context(nc.sbuf_tensor(pfx + "m1f", [128, 8, 1024], BF16))
            hb = [ps.enter_context(nc.sbuf_tensor(pfx + f"hb{i}", [128, NCC, 514], BF16))
                  for i in range(2)]
            zb = ps.enter_context(nc.sbuf_tensor(pfx + "zb", [128, NCC, 256], BF16))
            cvt = ps.enter_context(nc.sbuf_tensor(pfx + "cvt", [128, 256], BF16))
            nbf = ps.enter_context(nc.sbuf_tensor(pfx + "nb", [128, NCC, 256], BF16))
            vqb = ps.enter_context(nc.sbuf_tensor(pfx + "vqb", [128, 8, 256], F32))
            niqb = ps.enter_context(nc.sbuf_tensor(pfx + "niqb", [128, 4, 256], I32))
            ystg = [ps.enter_context(nc.sbuf_tensor(pfx + f"ystg{i}", [128, 1024], F32))
                    for i in range(2)]
            s2b = ps.enter_context(nc.sbuf_tensor(pfx + "s2b", [128, 256], F32))
            srow = ps.enter_context(nc.sbuf_tensor(pfx + "srow", [1, 256], F32))
            tq = ps.enter_context(nc.sbuf_tensor(pfx + "tq", [128, 8], F32))
            zc2t = [ps.enter_context(nc.sbuf_tensor(pfx + f"zc2{i}", [128, 1], F32))
                    for i in range(2)]
            rc2t = [ps.enter_context(nc.sbuf_tensor(pfx + f"rc2{i}", [128, 1], F32))
                    for i in range(2)]
            evsc = [[ps.enter_context(nc.sbuf_tensor(pfx + f"evsc{w}_{i}", [128, 1], F32))
                     for i in range(2)] for w in range(2)]
            xnb = ps.enter_context(nc.sbuf_tensor(pfx + "xnb", [128, 1024], F32))
            vxb = xnb
            nxb = ps.enter_context(nc.sbuf_tensor(pfx + "nxb", [128, 512], I32))
            pw1 = [ps.enter_context(nc.psum_tensor(pfx + f"pw1{i}", [128, 512], F32))
                   for i in range(2)]
            pw2 = [ps.enter_context(nc.psum_tensor(pfx + f"pw2{i}", [128, 512], F32))
                   for i in range(4)]
            ptr = ps.enter_context(nc.psum_tensor(pfx + "ptr", [128, 1024], BF16))
            paux = ps.enter_context(nc.psum_tensor(pfx + "paux", [128, 512], F32))

            xl_n = {}
            xr_n = {}
            yst_n = {}
            v_m1e = {}
            hev_s = {}
            halo_l = {}
            halo_r = {}
            p_m1t = {}
            p_w1done = {}
            v_yev = {}
            w1_arr = {}
            w2_arr = {}
            w1_slot_free = {s: 0 for s in range(8)}
            ptr_free = [(cv, 0)]
            paux_free = [0]
            pw1_free = {0: 0, 1: 0, 2: 0, 3: 0}
            pw2_free = [0]
            vq_last = [0]
            v_trees_done = [0]
            yst_prev = [0]
            ystats_n = {}
            pend_stats = []
            NW1B = NB * 8
            NW2B = NB * 2 * 8

            def issue_w1(g, gate):
                if g >= NW1B:
                    return
                bi, bidx = g // 8, g % 8
                cc0 = 4 * bidx
                s0 = (4 * bidx) % 8
                sm = sw1[g % 2]
                rec.emit("sync", lambda e, cc0=cc0, s0=s0: e.dma_start(
                    out=w1r[:, s0:s0 + 4, :, :].rearrange("p c a b -> p c (a b)"),
                    in_=w1q_h[cc0:cc0 + 4].rearrange("c p n -> p c n")),
                    waits=[(cp, gate)], inc=sm, by=16)
                w1_arr[(bi, bidx)] = (sm, sm.n)

            def issue_w2(g, gate):
                if g >= NW2B:
                    return
                wi, bidx = g // 8, g % 8
                cc0 = 4 * bidx
                s0 = (4 * bidx) % 8
                sm = sw2[g % 2]
                rec.emit("sync", lambda e, cc0=cc0, s0=s0: e.dma_start(
                    out=w2r[:, s0:s0 + 4, :],
                    in_=w2q_h[cc0:cc0 + 4].rearrange("c p n -> p c n")),
                    waits=[(cp, gate)], inc=sm, by=16)
                w2_arr[(wi, bidx)] = (sm, sm.n)

            def emit_lnq(b, js=(0, 1, 2, 3)):
                for j in js:
                    k = 4 * b + j
                    slsem, slval = xl_n[k]
                    rec.emit("scalar",
                             lambda e, k=k, d=xln[k % 3]: e.activation(
                                 xnb[:], d[:], AF.Identity,
                                 bias=mr_t[:, k:k + 1], scale=r_t[:, k:k + 1]),
                             waits=[(slsem, slval)])
                    s_ln = cs.n
                    rec.emit("vector", lambda e: e.tensor_reduce(
                        tq[:, 0:1], xnb[:], AX.X, OP.max,
                        apply_absolute_value=True), waits=[(cs, s_ln)])
                    if k + 3 < NTL:
                        gate_v = cv.n
                        rec.emit("sync",
                                 lambda e, k2=k + 3, d=xln[(k + 3) % 3]:
                                 e.dma_start(out=d[:],
                                             in_=src_h[k2 * 128:(k2 + 1) * 128, :]),
                                 waits=[(cv, gate_v)], inc=sxl[k % 3], by=16)
                        xl_n[k + 3] = (sxl[k % 3], sxl[k % 3].n)
                    rec.emit("vector", lambda e: e.tensor_single_scalar(
                        tq[:, 1:2], tq[:, 0:1], EPS, OP.max))
                    rec.emit("vector", lambda e: e.reciprocal(tq[:, 2:3], tq[:, 1:2]))
                    rec.emit("vector", lambda e: e.tensor_scalar(
                        vxb[:], xnb[:], tq[:, 2:3], 127.0, OP.mult, OP.mult))
                    v_vq = cv.n
                    wm = [(cp, p_m1t[k - 2])] if (k - 2) in p_m1t else []
                    for hx in range(2):
                        rec.emit("vector", lambda e, hx=hx: e.tensor_copy(
                            nxb[:], vxb[:, hx * 512:(hx + 1) * 512]))
                        rec.emit("vector",
                                 lambda e, hx=hx, d=m1tm[k % 2]: e.tensor_scalar(
                                     d[:, hx * 512:(hx + 1) * 512], nxb[:],
                                     tq[:, 1:2], 1.0 / 127.0, OP.mult, OP.mult),
                                 waits=(wm if hx == 0 else ()))
                    v_lnq = cv.n
                    for dc in range(NDC):
                        w = [(cv, v_lnq), ptr_free[0]] if dc == 0 else []
                        rec.emit("tensor",
                                 lambda e, dc=dc, sx=m1tm[k % 2]: e.transpose(
                                     ptr[:, dc * 128:(dc + 1) * 128],
                                     sx[:, dc * 128:(dc + 1) * 128], identb[:]),
                                 waits=w, inc=(cp if dc == NDC - 1 else None))
                    p_m1t[k] = cp.n
                    c0 = 128 * (k % 8)
                    wm = [(cp, p_w1done[b - 2])] if (b - 2) in p_w1done else []
                    rec.emit("scalar",
                             lambda e, c0=c0: e.copy(
                                 m1f[:, :, c0:c0 + 128],
                                 ptr[:].rearrange("p (a b) -> p a b", b=128)),
                             waits=[(cp, p_m1t[k])] + wm)
                    v_m1e[k] = (cs.h, cs.n)
                    ptr_free[0] = (cs, cs.n)

            pool_cc_n = {}

            def emit_conv_cc_pool(cc, hbuf, base, first_waits,
                                  defer_silu=False):
                wsl = [cwb[:, cc, j:j + 1].unsqueeze(1).broadcast_to([128, 256, 1])
                       for j in range(3)]
                rec.emit("gpsimd",
                         lambda e, cc=cc, hbuf=hbuf, base=base, w=wsl[0]:
                         e.tensor_tensor(
                             zb[:, cc, :].unsqueeze(2),
                             hbuf[:, cc, base - 1:base + 255].unsqueeze(2),
                             w, OP.mult), waits=first_waits)
                rec.emit("gpsimd",
                         lambda e, cc=cc, hbuf=hbuf, base=base, w=wsl[1]:
                         e.tensor_tensor(
                             cvt[:].unsqueeze(2),
                             hbuf[:, cc, base:base + 256].unsqueeze(2),
                             w, OP.mult))
                rec.emit("gpsimd", lambda e, cc=cc: e.tensor_tensor(
                    zb[:, cc, :], zb[:, cc, :], cvt[:], OP.add))
                rec.emit("gpsimd",
                         lambda e, cc=cc, hbuf=hbuf, base=base, w=wsl[2]:
                         e.tensor_tensor(
                             cvt[:].unsqueeze(2),
                             hbuf[:, cc, base + 1:base + 257].unsqueeze(2),
                             w, OP.mult))
                rec.emit("gpsimd", lambda e, cc=cc: e.tensor_tensor(
                    zb[:, cc, :], zb[:, cc, :], cvt[:], OP.add))
                pool_cc_n[cc] = cg.n
                if not defer_silu:
                    rec.emit("scalar",
                             lambda e, cc=cc: e.activation(
                                 zb[:, cc, :], zb[:, cc, :], AF.Silu,
                                 bias=cbv[:, cc:cc + 1]),
                             waits=[(cg, cg.n)])

            w1banks = [pw1[0], pw1[1], pw2[0], pw2[1]]

            def emit_w1_mm(b, cc, bank_i, extra_w=()):
                mcol = 512 * (b % 2)
                slot = cc % 8
                bank = w1banks[bank_i]
                wsem, wval = w1_arr[(b, cc // 4)]
                mec, men = v_m1e[4 * b + 3]
                w0 = [(wsem, wval), (cs, men),
                      (cs, pw1_free[bank_i])] + list(extra_w)
                for dc in range(NDC):
                    rec.emit("tensor",
                             lambda e, cc=cc, dc=dc, slot=slot, mcol=mcol,
                             bank=bank:
                             e.matmul(bank[:],
                                      w1r[:, slot, dc, :],
                                      m1f[:, dc, mcol:mcol + 512],
                                      start=(dc == 0),
                                      stop=(dc == NDC - 1)),
                             waits=(w0 if dc == 0 else ()),
                             inc=(cp if dc == NDC - 1 else None))
                pcc = cp.n
                w1_slot_free[slot] = pcc
                if cc % 4 == 3:
                    issue_w1(b * 8 + cc // 4 + 2, pcc)
                rec.emit("scalar",
                         lambda e, cc=cc, hb_=hb[b % 2], bank=bank: e.copy(
                             hb_[:, cc, 1:513], bank[:]),
                         waits=[(cp, pcc)])
                pw1_free[bank_i] = cs.n

            def emit_w1_convA(b, bw, yevac_cb):
                hbuf = hb[bw % 2] if bw >= 0 else None
                base = 1
                if bw >= 0:
                    for cc in range(NCONV_A, NCC):
                        fw = []
                        if cc == NCONV_A:
                            fw = [(cs, hev_s.get(bw, 0)),
                                  (cv, halo_l.get(bw, 0)),
                                  (cv, v_trees_done[0])]
                        emit_conv_cc_pool(cc, hbuf, base, fw, defer_silu=True)
                for cc in range(NCC):
                    if bw >= 0 and cc < NCC - NCONV_A:
                        pcc_ = NCONV_A + cc
                        rec.emit("scalar",
                                 lambda e, cc=pcc_: e.activation(
                                     zb[:, cc, :], zb[:, cc, :], AF.Silu,
                                     bias=cbv[:, cc:cc + 1]),
                                 waits=[(cg, pool_cc_n[pcc_])])
                    if b < NB and cc < 16:
                        emit_w1_mm(b, cc, cc % 2)
                    if cc == 16 and yevac_cb is not None:
                        yevac_cb()
                    if bw >= 0:
                        if cc >= NCONV_A:
                            continue
                        wc = []
                        if cc == 0:
                            wc = [(cs, hev_s.get(bw, 0)), (cg, vq_last[0])]
                        rec.emit("vector",
                                 lambda e, cc=cc, hbuf=hbuf: e.tensor_scalar(
                                     zb[:, cc, :], hbuf[:, cc, base - 1:base + 255],
                                     cwq[:, cc, 0:1], None, OP.mult), waits=wc)
                        rec.emit("vector",
                                 lambda e, cc=cc, hbuf=hbuf: e.scalar_tensor_tensor(
                                     zb[:, cc, :], hbuf[:, cc, base:base + 256],
                                     cwq[:, cc, 1:2], zb[:, cc, :], OP.mult, OP.add))
                        rec.emit("vector",
                                 lambda e, cc=cc, hbuf=hbuf: e.scalar_tensor_tensor(
                                     zb[:, cc, :], hbuf[:, cc, base + 1:base + 257],
                                     cwq[:, cc, 2:3], zb[:, cc, :], OP.mult, OP.add))
                        rec.emit("scalar",
                                 lambda e, cc=cc: e.activation(
                                     zb[:, cc, :], zb[:, cc, :], AF.Silu,
                                     bias=cbv[:, cc:cc + 1]),
                                 waits=[(cv, cv.n)])
                return cs.n

            def emit_w1_convB(b):
                if b >= NB:
                    return
                for cc in range(16, NCC):
                    extra = [(cv, yst_prev[0])] if cc in (18, 19) and \
                        yst_prev[0] > 0 else ()
                    emit_w1_mm(b, cc, cc % 4, extra)
                hev_s[b] = cs.n
                p_w1done[b] = cp.n

            def emit_halos(b):
                if b > NB:
                    return
                if b < NB:
                    if b == 0:
                        rec.emit("vector", lambda e: e.memset(hb[0][:, :, 0:1], 0.0))
                    else:
                        rec.emit("vector",
                                 lambda e, dst=hb[b % 2], src=hb[(b - 1) % 2]:
                                 e.tensor_copy(dst[:, :, 0:1], src[:, :, 512:513]),
                                 waits=[(cs, hev_s[b])])
                    halo_l[b] = cv.n
                if b >= 1:
                    if b < NB:
                        rec.emit("vector",
                                 lambda e, dst=hb[(b - 1) % 2], src=hb[b % 2]:
                                 e.tensor_copy(dst[:, :, 513:514], src[:, :, 1:2]),
                                 waits=[(cs, hev_s[b])])
                    else:
                        rec.emit("vector", lambda e: e.memset(
                            hb[(NB - 1) % 2][:, :, 513:514], 0.0))
                    halo_r[b - 1] = cv.n

            def emit_conv_silu(bw, wdx):
                hbuf = hb[bw % 2]
                base = 1 + 256 * wdx
                for cc in range(NCONV_B, NCC):
                    fw = []
                    if cc == NCONV_B:
                        fw = [(cs, hev_s.get(bw, 0)),
                              (cv, halo_r.get(bw, 0))]
                    emit_conv_cc_pool(cc, hbuf, base, fw, defer_silu=True)
                for cc in range(NCC):
                    if cc < NCC - NCONV_B:
                        pcc_ = NCONV_B + cc
                        rec.emit("scalar",
                                 lambda e, cc=pcc_: e.activation(
                                     zb[:, cc, :], zb[:, cc, :], AF.Silu,
                                     bias=cbv[:, cc:cc + 1]),
                                 waits=[(cg, pool_cc_n[pcc_])])
                    if cc >= NCONV_B:
                        continue
                    w0 = []
                    if cc == 0:
                        w0 = [(cs, hev_s.get(bw, 0)), (cg, vq_last[0])]
                    rec.emit("vector",
                             lambda e, cc=cc, base=base, hbuf=hbuf:
                             e.tensor_scalar(
                                 zb[:, cc, :], hbuf[:, cc, base - 1:base + 255],
                                 cwq[:, cc, 0:1], None, OP.mult), waits=w0)
                    rec.emit("vector",
                             lambda e, cc=cc, base=base, hbuf=hbuf:
                             e.scalar_tensor_tensor(
                                 zb[:, cc, :], hbuf[:, cc, base:base + 256],
                                 cwq[:, cc, 1:2], zb[:, cc, :], OP.mult, OP.add))
                    rec.emit("vector",
                             lambda e, cc=cc, base=base, hbuf=hbuf:
                             e.scalar_tensor_tensor(
                                 zb[:, cc, :], hbuf[:, cc, base + 1:base + 257],
                                 cwq[:, cc, 2:3], zb[:, cc, :], OP.mult, OP.add))
                    rec.emit("scalar",
                             lambda e, cc=cc: e.activation(
                                 zb[:, cc, :], zb[:, cc, :], AF.Silu,
                                 bias=cbv[:, cc:cc + 1]),
                             waits=[(cv, cv.n)])
                return cs.n

            def emit_trees_scale_quant(bw, wdx, s_silu, tile_cb=None):
                tree = [(0, 16, None, OP.max), (16, 24, (0, 16), OP.max),
                        (24, 28, (16, 24), OP.max), (28, 30, (24, 28), OP.max),
                        (30, 31, (28, 30), OP.max),
                        (0, 16, None, OP.min), (16, 24, (0, 16), OP.min),
                        (24, 28, (16, 24), OP.min), (28, 30, (24, 28), OP.min),
                        (31, 32, (28, 30), OP.min)]
                first = True
                for d0, d1, srcr, op in tree:
                    if srcr is None:
                        fn = (lambda e, d0=d0, d1=d1, op=op: e.tensor_tensor(
                            nbf[:, d0:d1, :], zb[:, 0::2, :], zb[:, 1::2, :], op))
                    else:
                        s0, s1 = srcr
                        fn = (lambda e, d0=d0, d1=d1, s0=s0, s1=s1, op=op:
                              e.tensor_tensor(nbf[:, d0:d1, :], nbf[:, s0:s1:2, :],
                                              nbf[:, s0 + 1:s1:2, :], op))
                    w = [(cs, s_silu), (cp, pw2_free[0])] if first else []
                    first = False
                    rec.emit("vector", fn, waits=w)
                v_tree = cv.n
                v_trees_done[0] = v_tree
                trsp = [(30, 0, 0), (30, 128, 128), (31, 0, 256), (31, 128, 384)]
                for i, (scc, soff, doff) in enumerate(trsp):
                    w = [(cv, v_tree), ptr_free[0]] if i == 0 else []
                    rec.emit("tensor",
                             lambda e, scc=scc, soff=soff, doff=doff:
                             e.transpose(ptr[:, doff:doff + 128],
                                         nbf[:, scc, soff:soff + 128], identb[:]),
                             waits=w, inc=(cp if i == 3 else None))
                p_trw = cp.n
                for ti in range(2):
                    rec.emit("vector", lambda e, ti=ti: e.tensor_reduce(
                        tq[:, 3:4], ptr[:, ti * 128:(ti + 1) * 128], AX.X, OP.max),
                        waits=([(cp, p_trw)] if ti == 0 else ()))
                    rec.emit("vector", lambda e, ti=ti: e.tensor_reduce(
                        tq[:, 4:5], ptr[:, 256 + ti * 128:384 + ti * 128],
                        AX.X, OP.min))
                    rec.emit("vector", lambda e: e.tensor_single_scalar(
                        tq[:, 4:5], tq[:, 4:5], -1.0, OP.mult))
                    rec.emit("vector", lambda e: e.tensor_tensor(
                        tq[:, 3:4], tq[:, 3:4], tq[:, 4:5], OP.max))
                    rec.emit("vector", lambda e, d=zc2t[ti]: e.tensor_single_scalar(
                        d[:], tq[:, 3:4], EPS, OP.max))
                    rec.emit("vector", lambda e, d=evsc[wdx][ti], z=zc2t[ti]:
                             e.tensor_scalar(d[:], z[:], hs2[:, 0:1], None, OP.mult))
                    rec.emit("vector", lambda e, d=rc2t[ti], z=zc2t[ti]:
                             e.reciprocal(d[:], z[:]))
                v_zc = cv.n
                ptr_free[0] = (cv, cv.n)
                for ti in range(2):
                    w = [(cv, v_zc), (cv, paux_free[0])] if ti == 0 else []
                    rec.emit("tensor", lambda e, ti=ti, z=rc2t[ti]: e.transpose(
                        paux[0:1, ti * 128:(ti + 1) * 128], z[:], identf[:]),
                        waits=w, inc=(cp if ti == 1 else None))
                p_srt = cp.n
                rec.emit("vector", lambda e: e.tensor_copy(srow[:], paux[0:1, 0:256]),
                         waits=[(cp, p_srt)])
                v_srow = cv.n
                rec.emit("tensor", lambda e: e.matmul(
                    paux[:, 256:512], ones_r[:], srow[:], start=True, stop=True),
                    waits=[(cv, v_srow)], inc=cp)
                p_bc = cp.n
                rec.emit("vector", lambda e: e.tensor_scalar(
                    s2b[:], paux[:, 256:512], 127.0, None, OP.mult),
                    waits=[(cp, p_bc)])
                v_s2b = cv.n
                paux_free[0] = cv.n
                i32_done = {}
                bf_done = {}
                vq_done = {}
                for g in range(8):
                    hfq = (g % 2) * 4
                    wq_ = [(cv, v_s2b)]
                    if g - 2 in i32_done:
                        wq_.append((cv, i32_done[g - 2]))
                    rec.emit("gpsimd", lambda e, g=g, hfq=hfq: e.tensor_tensor(
                        vqb[:, hfq:hfq + 4, :], zb[:, 4 * g:4 * g + 4, :],
                        s2b[:].unsqueeze(1).broadcast_to([128, 4, 256]),
                        OP.mult), waits=wq_)
                    vq_done[g] = cg.n
                    rec.emit("vector", lambda e, hfq=hfq: e.tensor_copy(
                        niqb[:], vqb[:, hfq:hfq + 4, :]),
                        waits=[(cg, vq_done[g])])
                    i32_done[g] = cv.n
                    rec.emit("vector", lambda e, g=g: e.tensor_copy(
                        nbf[:, 4 * g:4 * g + 4, :], niqb[:]))
                    bf_done[g] = cv.n
                    if tile_cb is not None and g % 2 == 1:
                        tile_cb()
                vq_last[0] = vq_done[7]
                return bf_done

            def emit_w2(bw, wdx, bf_done):
                wgl = 2 * bw + wdx
                for cc in range(NCC):
                    w2a = w2_arr[(wgl, cc // 4)]
                    for mi, (ti, hf) in enumerate([(0, 0), (0, 1), (1, 0), (1, 1)]):
                        w = []
                        if cc % 4 == 0 and mi == 0:
                            w = [(cv, bf_done[cc // 4])]
                        if cc == 0 and mi == 0:
                            if yst_prev[0] > 0:
                                w.append((cv, yst_prev[0]))
                            w.append((cs, max(pw1_free.values())))
                        if mi == 0:
                            w.append(w2a)
                        rec.emit("tensor",
                                 lambda e, cc=cc, ti=ti, hf=hf: e.matmul(
                                     pw2[2 * ti + hf][:],
                                     nbf[:, cc, ti * 128:(ti + 1) * 128],
                                     w2r[:, cc % 8, hf * 512:(hf + 1) * 512],
                                     start=(cc == 0), stop=(cc == NCC - 1)),
                                 waits=w,
                                 inc=(cp if (cc % 4 == 3 and mi == 3) else None))
                    if cc % 4 == 3:
                        issue_w2(wgl * 8 + cc // 4 + 2, cp.n)
                p_w2 = cp.n
                pw2_free[0] = p_w2
                return p_w2

            def emit_yevac(bw, wdx, p_w2):
                tok0 = 512 * bw + 256 * wdx
                for ti in range(2):
                    kk = tok0 // 128 + ti
                    xr_sem, xr_val = xr_n[kk]
                    ysl = kk % 2
                    w = [(cp, p_w2), (xr_sem, xr_val)]
                    stg = yst_n.get(kk - 2)
                    if stg is not None:
                        w.append(stg)
                    if kk - 2 in ystats_n:
                        w.append((cs, ystats_n[kk - 2]))
                    rec.emit("vector",
                             lambda e, ti=ti, ysl=ysl, kk=kk: e.scalar_tensor_tensor(
                                 ystg[ysl][:, 0:512], pw2[2 * ti + 0][:],
                                 evsc[wdx][ti][:, 0:1], xres[kk % 2][:, 0:512],
                                 OP.mult, OP.add), waits=w)
                    rec.emit("vector",
                             lambda e, ti=ti, ysl=ysl, kk=kk: e.scalar_tensor_tensor(
                                 ystg[ysl][:, 512:1024], pw2[2 * ti + 1][:],
                                 evsc[wdx][ti][:, 0:1], xres[kk % 2][:, 512:1024],
                                 OP.mult, OP.add))
                    v_ev = cv.n
                    v_yev[kk] = v_ev
                    wdma = [(cv, v_ev)]
                    if out_sum is not None:
                        def _stats(ysl=ysl, kk=kk, v_ev=v_ev):
                            rec.emit("scalar",
                                     lambda e: e.activation(
                                         sq_scr[:], ystg[ysl][:], AF.Identity,
                                         accum_out=out_sum[:, kk:kk + 1]),
                                     waits=[(cv, v_ev)])
                            rec.emit("scalar",
                                     lambda e: e.activation(
                                         sq_scr[:], ystg[ysl][:], AF.Square,
                                         accum_out=out_sq[:, kk:kk + 1]))
                            ystats_n[kk] = cs.n
                        pend_stats.append(_stats)
                    ssem = sst[ysl]
                    rec.emit("sync", lambda e, ysl=ysl, kk=kk: e.dma_start(
                        out=dst_h[kk * 128:(kk + 1) * 128, :], in_=ystg[ysl][:]),
                        waits=wdma, inc=ssem, by=16)
                    yst_n[kk] = (ssem, ssem.n)
                    if kk + 2 < NTL:
                        rec.emit("sync",
                                 lambda e, k2=kk + 2, d=xres[(kk + 2) % 2]:
                                 e.dma_start(out=d[:],
                                             in_=src_h[k2 * 128:(k2 + 1) * 128, :]),
                                 waits=[(cv, v_ev)], inc=sxr[(kk + 2) % 2], by=16)
                        xr_n[kk + 2] = (sxr[(kk + 2) % 2], sxr[(kk + 2) % 2].n)
                yst_prev[0] = cv.n

            for k in range(3):
                rec.emit("sync", lambda e, k=k, d=xln[k % 3]: e.dma_start(
                    out=d[:], in_=src_h[k * 128:(k + 1) * 128, :]),
                    inc=sxl[k % 3], by=16)
                xl_n[k] = (sxl[k % 3], sxl[k % 3].n)
            for kk in range(2):
                rec.emit("sync", lambda e, kk=kk, d=xres[kk % 2]: e.dma_start(
                    out=d[:], in_=src_h[kk * 128:(kk + 1) * 128, :]),
                    inc=sxr[kk % 2], by=16)
                xr_n[kk] = (sxr[kk % 2], sxr[kk % 2].n)
            issue_w1(0, 0)
            issue_w1(1, 0)
            issue_w2(0, 0)
            issue_w2(1, 0)

            emit_lnq(0)
            pend_w1 = [None]
            for b in range(NB + 2):
                bw = b - 1
                ycb = None
                if pend_w1[0] is not None:
                    pbw, pbf = pend_w1[0]
                    pw = emit_w2(pbw, 1, pbf)
                    ycb = (lambda pbw=pbw, pw=pw: emit_yevac(pbw, 1, pw))
                    pend_w1[0] = None
                if bw < NB:
                    s_silu0 = emit_w1_convA(b, bw, ycb)
                else:
                    s_silu0 = None
                    if ycb is not None:
                        ycb()
                pw0 = None
                ltiles = list(range(4)) if b + 1 < NB else []
                lt_i = [0]

                def tile_cb():
                    if lt_i[0] < len(ltiles):
                        emit_lnq(b + 1, (ltiles[lt_i[0]],))
                        lt_i[0] += 1
                if 0 <= bw < NB:
                    emit_w1_convB(b)
                    bf0 = emit_trees_scale_quant(bw, 0, s_silu0, tile_cb)
                    pw0 = emit_w2(bw, 0, bf0)
                else:
                    emit_w1_convB(b)
                for st_cb in pend_stats:
                    st_cb()
                del pend_stats[:]
                emit_halos(b)
                if 0 <= bw < NB:
                    s_silu1 = emit_conv_silu(bw, 1)
                    emit_yevac(bw, 0, pw0)
                    bf1 = emit_trees_scale_quant(bw, 1, s_silu1)
                    while lt_i[0] < len(ltiles):
                        tile_cb()
                    pend_w1[0] = (bw, bf1)
                elif b + 1 < NB:
                    emit_lnq(b + 1)
                for st_cb in pend_stats:
                    st_cb()
                del pend_stats[:]

            for ssem in sst:
                rec.emit("sync", None, waits=[(ssem, ssem.n)])
            if out_sum is not None:
                rstd_next[0](rec)
            phase_barrier(rec)
            with nc.Block(name) as blk:
                rec.replay(blk)


    rstd_next = [None]
    rstd_next[0] = lambda rec: rstd_batch(rec, stB_s, stB_q, mB, rB)
    ffn_phase(x_h, y1_h, mA, rA, mrA, stB_s, stB_q, "passA")

    rec = Rec(chain={"vector": cv, "scalar": cs, "gpsimd": cg})
    with contextlib.ExitStack() as ps:
        wq = [ps.enter_context(nc.sbuf_tensor(f"b_wq{i}", [128, 3072], BF16))
              for i in range(NDC)]
        Ssb = [ps.enter_context(nc.sbuf_tensor(f"b_S{i}", [128, 1024], BF16))
               for i in range(NDC)]
        ylr = [ps.enter_context(nc.sbuf_tensor(f"b_yl{i}", [128, 1024], F32))
               for i in range(8)]
        xn2 = [ps.enter_context(nc.sbuf_tensor(f"b_xn2{i}", [128, 1024], BF16))
               for i in range(2)]
        xn2f = [ps.enter_context(nc.sbuf_tensor(f"b_xn2f{i}", [128, 8, 512],
                                                 BF16)) for i in range(2)]
        qf = [ps.enter_context(nc.sbuf_tensor(f"b_qf{i}", [128, 8, 512], BF16))
              for i in range(2)]
        kf = [ps.enter_context(nc.sbuf_tensor(f"b_kf{i}", [128, 8, 512], BF16))
              for i in range(2)]
        vf = ps.enter_context(nc.sbuf_tensor("b_vf", [128, 8, 512], BF16))
        vtm = [ps.enter_context(nc.sbuf_tensor(f"b_vtm{i}", [128, 1024], BF16))
               for i in range(4)]
        ktm = [ps.enter_context(nc.sbuf_tensor(f"b_ktm{i}", [128, 1024], BF16))
               for i in range(4)]
        sT = [ps.enter_context(nc.sbuf_tensor(f"b_sT{i}", [128, 512], BF16))
              for i in range(4)]
        y2st = [ps.enter_context(nc.sbuf_tensor(f"b_y2st{i}", [128, 1024], F32))
                for i in range(2)]
        pq = [ps.enter_context(nc.psum_tensor(f"b_pq{i}", [128, 512], F32))
              for i in range(2)]
        psc = ps.enter_context(nc.psum_tensor("b_psc", [128, 512], F32))
        pro = [ps.enter_context(nc.psum_tensor(f"b_pro{i}", [128, 512], F32))
               for i in range(2)]
        pst = [ps.enter_context(nc.psum_tensor(f"b_pst{i}", [128, 512], F32))
               for i in range(2)]
        ptrB = ps.enter_context(nc.psum_tensor("b_ptrB", [128, 1024], BF16))
        ptrB2 = ptrB

        for dc in range(NDC):
            rec.emit("gpsimd", lambda e, dc=dc: e.dma_start(
                out=wq[dc][:], in_=wqkvT_h[dc * 128:(dc + 1) * 128, :]),
                inc=swq, by=16)
        n_wq = swq.n
        for dc in range(NDC):
            rec.emit("vector", lambda e, dc=dc: e.memset(Ssb[dc][:], 0.0),
                     inc=(cv if dc == NDC - 1 else None))
        v_supd = cv.n

        yl_n = {}
        y2st_n = {}
        st_upd_v = {}
        v_y2ev = {}
        s_qkv_ev = {}
        p_tr_tile = {}
        p_qkv_done = {}
        pst_free = {0: 0, 1: 0}
        ptrB_free = [0]
        ptrB2_free = [0]
        for k in range(8):
            rec.emit("sync", lambda e, k=k, d=ylr[k % 8]: e.dma_start(
                out=d[:], in_=y1_h[k * 128:(k + 1) * 128, :]),
                inc=syl[k % 8], by=16)
            yl_n[k] = (syl[k % 8], syl[k % 8].n)

        v_xn2f = {}
        s_qkv_done = {}

        def emit_ln2(tc):
            for j in range(4):
                k = 4 * tc + j
                pb = ptrB
                pbf = ptrB_free
                slsem, slval = yl_n[k]
                wm = [(slsem, slval)]
                if k - 2 in p_tr_tile:
                    wm.append((cp, p_tr_tile[k - 2]))
                rec.emit("vector", lambda e, k=k, j=j, d=ylr[k % 8]:
                         e.tensor_scalar(xn2[j % 2][:], d[:], mB[:, k:k + 1],
                                         rB[:, k:k + 1], OP.subtract, OP.mult),
                         waits=wm, inc=cv)
                v_ln2 = cv.n
                for dc in range(NDC):
                    w = []
                    if dc == 0:
                        w = [(cv, v_ln2), (cv, pbf[0])]
                    rec.emit("tensor", lambda e, dc=dc, sx=xn2[j % 2], pb=pb:
                             e.transpose(pb[:, dc * 128:(dc + 1) * 128],
                                         sx[:, dc * 128:(dc + 1) * 128],
                                         identb[:]),
                             waits=w, inc=(cp if dc == NDC - 1 else None))
                p_tr_tile[k] = cp.n
                wm = []
                if tc - 2 in p_qkv_done:
                    wm = [(cp, p_qkv_done[tc - 2])]
                rec.emit("vector", lambda e, j=j, pb=pb, x2=xn2f[tc % 2]:
                         e.tensor_copy(
                             x2[:, :, j * 128:(j + 1) * 128],
                             pb[:].rearrange("p (a b) -> p a b", b=128)),
                         waits=[(cp, p_tr_tile[k])] + wm, inc=cv)
                pbf[0] = cv.n
            v_xn2f[tc] = cv.n

        def emit_qkv_group(tc, oc):
            for dc in range(NDC):
                w = []
                if dc == 0:
                    w = [(cv, v_xn2f[tc]), (swq, n_wq)]
                    if oc >= 2:
                        w.append((cs, s_qkv_ev[(tc, oc - 2)]))
                    elif (tc - 1, 22 + oc) in s_qkv_ev:
                        w.append((cs, s_qkv_ev[(tc - 1, 22 + oc)]))
                rec.emit("tensor", lambda e, tc=tc, oc=oc, dc=dc: e.matmul(
                    pq[oc % 2][:], wq[dc][:, oc * 128:(oc + 1) * 128],
                    xn2f[tc % 2][:, dc, :],
                    start=(dc == 0), stop=(dc == NDC - 1)),
                    waits=w, inc=(cp if dc == NDC - 1 else None))
            p_q = cp.n
            dsts = [qf[tc % 2], kf[tc % 2], vf][oc // 8]
            rec.emit("scalar", lambda e, oc=oc, d=dsts: e.activation(
                d[:, oc % 8, :], pq[oc % 2][:], AF.Identity,
                bias=bqkv[:, oc:oc + 1]),
                waits=[(cp, p_q)], inc=cs)
            s_qkv_ev[(tc, oc)] = cs.n
            if oc == 23:
                p_qkv_done[tc] = cp.n
                s_qkv_done[tc] = cs.n

        emit_ln2(0)
        for oc in range(24):
            emit_qkv_group(0, oc)

        for c in range(NCHK):
            tposes = [(jb, which) for jb in range(4) for which in ("v", "k")]
            for idx, (jb, which) in enumerate(tposes):
                pb = ptrB
                pbf = ptrB_free
                src = vf if which == "v" else kf[c % 2]
                dst = vtm[jb] if which == "v" else ktm[jb]
                for dv in range(NDC):
                    w = []
                    if dv == 0:
                        w = [(cs, s_qkv_done[c]), (cv, pbf[0])]
                    rec.emit("tensor", lambda e, dv=dv, jb=jb, src=src, pb=pb:
                             e.transpose(pb[:, dv * 128:(dv + 1) * 128],
                                         src[:, dv, jb * 128:(jb + 1) * 128],
                                         identb[:]),
                             waits=w, inc=(cp if dv == NDC - 1 else None))
                p_tr = cp.n
                if which == "v":
                    rec.emit("vector", lambda e, dst=dst, pb=pb: e.tensor_copy(
                        dst[:], pb[:]), waits=[(cp, p_tr)], inc=cv)
                else:
                    rec.emit("vector", lambda e, dst=dst, jb=jb, pb=pb:
                             e.tensor_scalar(dst[:], pb[:],
                                             kdec[:, jb:jb + 1], None,
                                             OP.mult),
                             waits=[(cp, p_tr)], inc=cv)
                pbf[0] = cv.n
            v_trans = cv.n
            if c + 1 < NCHK:
                emit_ln2(c + 1)
            v_sc_ev = cv.n
            for jb in range(4):
                ni = 512 - jb * 128
                for dc in range(NDC):
                    w = []
                    if dc == 0:
                        w = [(cv, v_sc_ev), (cv, v_trans)]
                    rec.emit("tensor",
                             lambda e, jb=jb, dc=dc, ni=ni, kk=kf[c % 2],
                             qq=qf[c % 2]: e.matmul(
                                 psc[:, 0:ni], kk[:, dc, jb * 128:(jb + 1) * 128],
                                 qq[:, dc, jb * 128:512],
                                 start=(dc == 0), stop=(dc == NDC - 1)),
                             waits=w, inc=(cp if dc == NDC - 1 else None))
                p_sc = cp.n
                aimm = float(ALPHA ** (-128.0 * jb))
                rec.emit("vector", lambda e, jb=jb, aimm=aimm:
                         e.scalar_tensor_tensor(sT[jb][:, 0:128], psc[:, 0:128],
                                                aimm, trik[:], OP.mult, OP.mult),
                         waits=[(cp, p_sc)])
                if ni > 128:
                    rec.emit("vector", lambda e, jb=jb, ni=ni, aimm=aimm:
                             e.tensor_scalar(sT[jb][:, 128:ni], psc[:, 128:ni],
                                             kcol[:, 0:1], aimm, OP.mult,
                                             OP.mult))
                rec.emit("vector", lambda e: None, inc=cv)
                v_sc_ev = cv.n
            v_scores = cv.n
            for it in range(4):
                k = 4 * c + it
                for hf in range(2):
                    mms = []
                    if c > 0:
                        mms += [("cross", dc) for dc in range(NDC)]
                    mms += [("av", jb) for jb in range(it + 1)]
                    for mi, (kind, idx) in enumerate(mms):
                        w = []
                        if mi == 0:
                            w = [(cv, v_scores), (cv, v_supd)]
                            prev = (v_y2ev.get((c, it - 1, hf)) if it > 0
                                    else v_y2ev.get((c - 1, 3, hf)))
                            if prev is not None:
                                w.append((cv, prev))
                        st = (mi == 0)
                        sp = (mi == len(mms) - 1)
                        if kind == "cross":
                            rec.emit("tensor",
                                     lambda e, it=it, hf=hf, dc=idx, st=st, sp=sp,
                                     qq=qf[c % 2]:
                                     e.matmul(pro[hf][:],
                                              qq[:, dc, it * 128:(it + 1) * 128],
                                              Ssb[dc][:, hf * 512:(hf + 1) * 512],
                                              start=st, stop=sp),
                                     waits=w, inc=(cp if sp else None))
                        else:
                            rec.emit("tensor",
                                     lambda e, it=it, hf=hf, jb=idx, st=st, sp=sp:
                                     e.matmul(
                                         pro[hf][:],
                                         sT[jb][:, (it - jb) * 128:
                                                (it - jb + 1) * 128],
                                         vtm[jb][:, hf * 512:(hf + 1) * 512],
                                         start=st, stop=sp),
                                     waits=w, inc=(cp if sp else None))
                    p_ro = cp.n
                    ysl = k % 2
                    slsem, slval = yl_n[k]
                    w = [(cp, p_ro), (slsem, slval)]
                    stg = y2st_n.get(k - 2)
                    if stg is not None:
                        w.append(stg)
                    rec.emit("vector",
                             lambda e, it=it, hf=hf, ysl=ysl, k=k:
                             e.scalar_tensor_tensor(
                                 y2st[ysl][:, hf * 512:(hf + 1) * 512],
                                 pro[hf][:], evk[:, it:it + 1],
                                 ylr[k % 8][:, hf * 512:(hf + 1) * 512],
                                 OP.mult, OP.add),
                             waits=w, inc=cv)
                    v_y2ev[(c, it, hf)] = cv.n
                    if c + 1 < NCHK:
                        g0 = 3 * (2 * it + hf)
                        for oc in range(g0, g0 + 3):
                            emit_qkv_group(c + 1, oc)
                v_ev = cv.n
                rec.emit("vector", lambda e, ysl=k % 2, k=k: e.tensor_reduce(
                    stC_s[:, k:k + 1], y2st[ysl][:], AX.X, OP.add), inc=cv)
                rec.emit("scalar", lambda e, ysl=k % 2, k=k: e.activation(
                    sq_scr[:], y2st[ysl][:], AF.Square,
                    accum_out=stC_q[:, k:k + 1]),
                    waits=[(cv, cv.n)], inc=cs)
                ssem = sst[k % 2]
                rec.emit("sync", lambda e, ysl=k % 2, k=k: e.dma_start(
                    out=y2_h[k * 128:(k + 1) * 128, :], in_=y2st[ysl][:]),
                    waits=[(cv, cv.n), (cs, cs.n)], inc=ssem, by=16)
                y2st_n[k] = (ssem, ssem.n)
                if k + 8 < NTL:
                    rec.emit("sync", lambda e, k2=k + 8, d=ylr[(k + 8) % 8]:
                             e.dma_start(out=d[:],
                                         in_=y1_h[k2 * 128:(k2 + 1) * 128, :]),
                             waits=[(cv, v_ev)], inc=syl[(k + 8) % 8], by=16)
                    yl_n[k + 8] = (syl[(k + 8) % 8], syl[(k + 8) % 8].n)
            p_retout_all = cp.n
            if c < NCHK - 1:
                for dc in range(NDC):
                    for hf in range(2):
                        bank = hf
                        for jb in range(4):
                            w = []
                            if jb == 0:
                                w = [(cv, v_trans)]
                                if pst_free[bank] > 0:
                                    w.append((cv, pst_free[bank]))
                            rec.emit("tensor",
                                     lambda e, dc=dc, hf=hf, jb=jb, bank=bank:
                                     e.matmul(
                                         pst[bank][:],
                                         ktm[jb][:, dc * 128:(dc + 1) * 128],
                                         vtm[jb][:, hf * 512:(hf + 1) * 512],
                                         start=(jb == 0), stop=(jb == 3)),
                                     waits=w, inc=(cp if jb == 3 else None))
                        p_st = cp.n
                        rec.emit("vector",
                                 lambda e, dc=dc, hf=hf, bank=bank:
                                 e.scalar_tensor_tensor(
                                     Ssb[dc][:, hf * 512:(hf + 1) * 512],
                                     Ssb[dc][:, hf * 512:(hf + 1) * 512], A512,
                                     pst[bank][:], OP.mult, OP.add),
                                 waits=[(cp, p_st), (cp, p_retout_all)], inc=cv)
                        pst_free[bank] = cv.n
                v_supd = cv.n
        for ssem in sst:
            rec.emit("sync", None, waits=[(ssem, ssem.n)])
        rstd_batch(rec, stC_s, stC_q, mC, rC, mrC)
        phase_barrier(rec)
        with nc.Block("passB") as blk:
            rec.replay(blk)

    ffn_phase(y2_h, out_h, mC, rC, mrC, None, None, "passC")

    rec = Rec(chain={"vector": cv, "scalar": cs, "gpsimd": cg})
    for ssem in sst:
        rec.emit("sync", None, waits=[(ssem, ssem.n)])
    with nc.Block("fin") as blk:
        rec.replay(blk)

    es.close()
    return nc


def _lay(a):
    return np.ascontiguousarray(a)


_NC_CACHE = {}


def kernel(L0_attn_x, L0_attn_W_qkv, L0_attn_b_qkv, L0_ffn_w1, L0_ffn_conv_w,
           L0_ffn_conv_b, L0_ffn_w2):
    x = np.asarray(L0_attn_x, dtype=np.float32)
    Wqkv = np.asarray(L0_attn_W_qkv, dtype=np.float32)
    bq = np.asarray(L0_attn_b_qkv, dtype=np.float32)
    w1 = np.asarray(L0_ffn_w1, dtype=np.float32)
    cw = np.asarray(L0_ffn_conv_w, dtype=np.float32)
    cb = np.asarray(L0_ffn_conv_b, dtype=np.float32)
    w2 = np.asarray(L0_ffn_w2, dtype=np.float32)

    w1s = _lay(w1.T.reshape(8, 128, 32, 128).transpose(2, 1, 0, 3)
               .reshape(32, 128, 1024))
    w2s = _lay(w2.T.reshape(32, 128, 1024))
    wqkvT = _lay(Wqkv.T)
    bq_l = _lay(bq.reshape(24, 128).T)
    cw_l = _lay(cw.reshape(32, 128, 3).transpose(1, 0, 2))
    cb_l = _lay(cb.reshape(32, 128).T)

    p = np.arange(128, dtype=np.float64)
    identb = np.eye(128, dtype=bfdt)
    identf = np.eye(128, dtype=np.float32)
    kcol = (ALPHA ** (-(p + 1))).astype(np.float32).reshape(128, 1)
    ii = np.arange(128, dtype=np.float64)
    trik = np.where(ii[None, :] >= p[:, None],
                    (ALPHA ** (-(p + 1)))[:, None], 0.0).astype(np.float32)
    evk = np.stack([(ALPHA ** (k * 128 + p + 1)) / 32.0 for k in range(4)],
                   axis=1).astype(np.float32)
    kdec = np.stack([ALPHA ** (CH - 1 - (k * 128 + p)) for k in range(4)],
                    axis=1).astype(np.float32)

    if "nc" not in _NC_CACHE:
        _NC_CACHE["nc"] = build_program()
    nc = _NC_CACHE["nc"]

    in_maps = [{
        "x": _lay(x[i]), "w1s": w1s, "w2s": w2s, "wqkvT": wqkvT,
        "bqkvL": bq_l, "cwL": cw_l, "cbL": cb_l, "identb": identb,
        "identf": identf, "kcol": kcol, "trik": trik, "evk": evk, "kdec": kdec,
    } for i in range(8)]
    res = run_bass_kernel_spmd(nc, in_maps, core_ids=list(range(8)))
    return np.stack([r["out"] for r in res.results], axis=0)


# version 46 (final)
# speedup vs baseline: 1.0231x; 1.0231x over previous
import contextlib
import numpy as np
import ml_dtypes
import concourse.bass as bass
import concourse.mybir as mybir
from concourse.bass_utils import run_bass_kernel_spmd

F32 = mybir.dt.float32
BF16 = mybir.dt.bfloat16
I32 = mybir.dt.int32
AF = mybir.ActivationFunctionType
OP = mybir.AluOpType
AX = mybir.AxisListType
bfdt = ml_dtypes.bfloat16

ALPHA = 0.99
S = 8192
D = 1024
HID = 4096
NCC = 32
NDC = 8
CH = 512
NCHK = S // CH
TB = 512
NB = S // TB
NTL = S // 128
EPS = 1e-5
A512 = float(ALPHA ** CH)

ENGS = ("tensor", "vector", "scalar", "gpsimd", "sync")
NCONV_V = 22
NCONV_A = 24
NCONV_B = 21


class Ctr:
    __slots__ = ("h", "n")

    def __init__(self, h):
        self.h = h
        self.n = 0


class Rec:

    def __init__(self, chain=None):
        self.ops = {e: [] for e in ENGS}
        self.chain = chain or {}

    def emit(self, eng, fn, waits=(), inc=None, by=1):
        w = [(c.h, int(v)) for c, v in waits if v > 0]
        ch = self.chain.get(eng)
        if fn is not None and by != 16 and ch is not None:
            if ch.n > 0:
                w.append((ch.h, ch.n))
            assert inc is None or inc is ch
            inc = ch
            by = 1
        if inc is not None:
            inc.n += by
        self.ops[eng].append((tuple(w), fn, (inc.h if inc is not None else None),
                              by))
        return inc.n if inc is not None else None

    def replay(self, block):
        def mk(name):
            ops = self.ops[name]

            def run(e):
                for w, fn, inch, by in ops:
                    for h, v in w:
                        e.wait_ge(h, v)
                    ins = fn(e) if fn is not None else None
                    if inch is not None:
                        if ins is None:
                            ins = e.nop()
                        ins.then_inc(inch, by)
            return run
        block.tensor(mk("tensor"))
        block.vector(mk("vector"))
        block.scalar(mk("scalar"))
        block.gpsimd(mk("gpsimd"))
        block.sync(mk("sync"))


def build_program():
    nc = bass.Bass("TRN2", target_bir_lowering=False)

    x_h = nc.dram_tensor("x", [S, D], F32, kind="ExternalInput").ap()
    w1s_h = nc.dram_tensor("w1s", [NCC, 128, 1024], F32, kind="ExternalInput").ap()
    w2s_h = nc.dram_tensor("w2s", [NCC, 128, 1024], F32, kind="ExternalInput").ap()
    wqkvT_h = nc.dram_tensor("wqkvT", [D, 3 * D], F32, kind="ExternalInput").ap()
    bqkv_h = nc.dram_tensor("bqkvL", [128, 24], F32, kind="ExternalInput").ap()
    cw_h = nc.dram_tensor("cwL", [128, NCC, 3], F32, kind="ExternalInput").ap()
    cb_h = nc.dram_tensor("cbL", [128, NCC], F32, kind="ExternalInput").ap()
    identb_h = nc.dram_tensor("identb", [128, 128], BF16, kind="ExternalInput").ap()
    identf_h = nc.dram_tensor("identf", [128, 128], F32, kind="ExternalInput").ap()
    kcol_h = nc.dram_tensor("kcol", [128, 1], F32, kind="ExternalInput").ap()
    trik_h = nc.dram_tensor("trik", [128, 128], F32, kind="ExternalInput").ap()
    evk_h = nc.dram_tensor("evk", [128, 4], F32, kind="ExternalInput").ap()
    kdec_h = nc.dram_tensor("kdec", [128, 4], F32, kind="ExternalInput").ap()
    out_h = nc.dram_tensor("out", [S, D], F32, kind="ExternalOutput").ap()
    y1_h = nc.dram_tensor("y1scr", [S, D], F32, kind="Internal").ap()
    y2_h = nc.dram_tensor("y2scr", [S, D], F32, kind="Internal").ap()
    w1q_h = nc.dram_tensor("w1q", [NCC, 128, 1024], BF16, kind="Internal").ap()
    w2q_h = nc.dram_tensor("w2q", [NCC, 128, 1024], BF16, kind="Internal").ap()

    es = contextlib.ExitStack()

    def sb(name, shape, dt):
        return es.enter_context(nc.sbuf_tensor(name, shape, dt))

    def sem(name):
        return Ctr(es.enter_context(nc.semaphore(name)))

    identb = sb("identb_s", [128, 128], BF16)
    identf = sb("identf_s", [128, 128], F32)
    kcol = sb("kcol_s", [128, 1], F32)
    trik = sb("trik_s", [128, 128], F32)
    evk = sb("evk_s", [128, 4], F32)
    kdec = sb("kdec_s", [128, 4], F32)
    bqkv = sb("bqkv_s", [128, 24], F32)
    cwq = sb("cwq_s", [128, NCC, 3], F32)
    cwb = sb("cwb_s", [128, NCC, 3], BF16)
    cbv = sb("cbv_s", [128, NCC], F32)
    ones_c = sb("ones_c", [128, 1], F32)
    ones_r = sb("ones_r", [1, 128], F32)
    m1c = sb("m1c_s", [128, 1], F32)
    s1pp = sb("s1pp_s", [128, 1], F32)
    m2c = sb("m2c_s", [128, 1], F32)
    s2pp = sb("s2pp_s", [128, 1], F32)
    hs2 = sb("hs2_s", [128, 1], F32)
    stA_s = sb("stA_s", [128, NTL], F32)
    stA_q = sb("stA_q", [128, NTL], F32)
    mA = sb("mA", [128, NTL], F32)
    mrA = sb("mrA", [128, NTL], F32)
    rA = sb("rA", [128, NTL], F32)
    stB_s = sb("stB_s", [128, NTL], F32)
    stB_q = sb("stB_q", [128, NTL], F32)
    mB = sb("mB", [128, NTL], F32)
    rB = sb("rB", [128, NTL], F32)
    stC_s = sb("stC_s", [128, NTL], F32)
    stC_q = sb("stC_q", [128, NTL], F32)
    mC = sb("mC", [128, NTL], F32)
    mrC = sb("mrC", [128, NTL], F32)
    rC = sb("rC", [128, NTL], F32)
    sq_scr = sb("sq_scr", [128, 1024], F32)
    stt_tmp = sb("stt_tmp", [128, NTL], F32)

    cv = sem("cv")
    cs = sem("cs")
    cp = sem("cp")
    cg = sem("cg")
    bar = sem("bar")
    sxl = [sem(f"sxl{i}") for i in range(6)]
    sxr = [sem(f"sxr{i}") for i in range(4)]
    sst = [sem(f"sst{i}") for i in range(2)]
    sw1 = [sem(f"sw1{i}") for i in range(2)]
    sw2 = [sem(f"sw2{i}") for i in range(2)]
    syl = [sem(f"syl{i}") for i in range(8)]
    spr = [sem(f"spr{i}") for i in range(4)]
    spx = [sem(f"spx{i}") for i in range(4)]
    swq = sem("swq")
    sqs = [sem(f"sqs{i}") for i in range(2)]
    ld_small = sem("ld_small")

    def phase_barrier(rec):
        bar.n += len(ENGS)
        tgt = bar.n
        for e in ENGS:
            rec.ops[e].append(((), (lambda en: en.sem_inc(bar.h, 1)), None, 0))
            rec.ops[e].append((((bar.h, tgt),), None, None, 0))

    def rstd_batch(rec, sum_t, sq_t, m_t, r_t, mr_t=None):
        rec.emit("vector", lambda e: e.tensor_scalar(
            m_t[:], sum_t[:], 1.0 / D, None, OP.mult),
            waits=[(cs, cs.n)])
        rec.emit("vector", lambda e: e.tensor_tensor(
            stt_tmp[:], m_t[:], m_t[:], OP.mult))
        rec.emit("vector", lambda e: e.scalar_tensor_tensor(
            stt_tmp[:], sq_t[:], 1.0 / D, stt_tmp[:], OP.mult, OP.subtract))
        rec.emit("vector", lambda e: e.tensor_scalar(
            stt_tmp[:], stt_tmp[:], EPS, None, OP.add), inc=cv)
        rec.emit("scalar", lambda e: e.activation(stt_tmp[:], stt_tmp[:], AF.Ln),
                 waits=[(cv, cv.n)])
        rec.emit("scalar", lambda e: e.activation(
            r_t[:], stt_tmp[:], AF.Exp, scale=-0.5), inc=cs)
        if mr_t is not None:
            rec.emit("vector", lambda e: e.scalar_tensor_tensor(
                mr_t[:], m_t[:], -1.0, r_t[:], OP.mult, OP.mult),
                waits=[(cs, cs.n)])
        rec.emit("vector", lambda e: None, waits=[(cs, cs.n)], inc=cv)

    rec = Rec(chain={"vector": cv, "scalar": cs, "gpsimd": cg})
    with contextlib.ExitStack() as ps:
        wtile = [ps.enter_context(nc.sbuf_tensor(f"p_w{i}", [128, 1024], F32))
                 for i in range(4)]
        qtile = [ps.enter_context(nc.sbuf_tensor(f"p_q{i}", [128, 1024], BF16))
                 for i in range(2)]
        racc = ps.enter_context(nc.sbuf_tensor("p_racc", [128, 1], F32))
        rtmp = ps.enter_context(nc.sbuf_tensor("p_rtmp", [128, 1], F32))
        vtmp = ps.enter_context(nc.sbuf_tensor("p_vtmp", [128, 1024], F32))
        itmp = ps.enter_context(nc.sbuf_tensor("p_itmp", [128, 1024], I32))
        sc11 = ps.enter_context(nc.sbuf_tensor("p_sc11", [1, 1], F32))
        xt = [ps.enter_context(nc.sbuf_tensor(f"p_x{i}", [128, 1024], F32))
              for i in range(4)]
        paux = ps.enter_context(nc.psum_tensor("p_paux", [128, 512], F32))

        for dst, src in [(identb, identb_h), (identf, identf_h), (kcol, kcol_h),
                         (trik, trik_h), (evk, evk_h), (kdec, kdec_h),
                         (bqkv, bqkv_h), (cwq, cw_h), (cbv, cb_h)]:
            rec.emit("sync", lambda e, d=dst, s=src: e.dma_start(out=d[:], in_=s),
                     inc=ld_small, by=16)
        n_small = ld_small.n
        rec.emit("vector", lambda e: e.memset(ones_c[:], 1.0))
        rec.emit("vector", lambda e: e.memset(ones_r[:], 1.0), inc=cv)

        def prep_weights(ws_h, wq_dst, mc_dst, spp_dst):
            red_done = {}
            gate0 = cv.n
            for t in range(min(4, NCC)):
                rec.emit("sync", lambda e, t=t, d=wtile[t % 4]: e.dma_start(
                    out=d[:], in_=ws_h[t]), waits=[(cv, gate0)],
                    inc=spr[t % 4], by=16)
                ld_n[("p1", t)] = (spr[t % 4], spr[t % 4].n)
            rec.emit("vector", lambda e: e.memset(racc[:], 0.0), inc=cv)
            for t in range(NCC):
                slsem, slval = ld_n[("p1", t)]
                rec.emit("vector",
                         lambda e, d=wtile[t % 4]: e.tensor_reduce(
                             rtmp[:], d[:], AX.X, OP.add,
                             apply_absolute_value=True),
                         waits=[(slsem, slval)])
                rec.emit("vector",
                         lambda e: e.tensor_tensor(racc[:], rtmp[:], racc[:],
                                                   OP.add), inc=cv)
                red_done[t] = cv.n
                if t + 4 < NCC:
                    rec.emit("sync", lambda e, t2=t + 4, d=wtile[(t + 4) % 4]:
                             e.dma_start(out=d[:], in_=ws_h[t2]),
                             waits=[(cv, red_done[t])], inc=spr[t % 4], by=16)
                    ld_n[("p1", t + 4)] = (spr[t % 4], spr[t % 4].n)
            v_abs = cv.n
            rec.emit("tensor", lambda e: e.matmul(
                paux[0:1, 0:1], racc[:], ones_c[:], start=True, stop=True),
                waits=[(cv, v_abs)], inc=cp)
            rec.emit("vector", lambda e: e.tensor_scalar(
                sc11[:], paux[0:1, 0:1], 1.0 / (HID * D), EPS, OP.mult, OP.max),
                waits=[(cp, cp.n)], inc=cv)
            v_sc = cv.n
            rec.emit("tensor", lambda e: e.matmul(
                paux[:, 0:1], ones_r[:], sc11[:], start=True, stop=True),
                waits=[(cv, v_sc)], inc=cp)
            rec.emit("vector", lambda e, d=mc_dst: e.tensor_copy(d[:], paux[:, 0:1]),
                     waits=[(cp, cp.n)])
            rec.emit("vector", lambda e, d=spp_dst, m=mc_dst: e.reciprocal(
                d[:], m[:]), inc=cv)
            v_spp = cv.n
            q_done = {}
            for t in range(min(4, NCC)):
                rec.emit("sync", lambda e, t=t, d=wtile[t % 4]: e.dma_start(
                    out=d[:], in_=ws_h[t]), waits=[(cv, v_spp)],
                    inc=spr[t % 4], by=16)
                ld_n[("p2", t)] = (spr[t % 4], spr[t % 4].n)
            for t in range(NCC):
                slsem, slval = ld_n[("p2", t)]
                rec.emit("vector",
                         lambda e, d=wtile[t % 4], s=spp_dst: e.tensor_scalar(
                             vtmp[:], d[:], s[:, 0:1], None, OP.mult),
                         waits=[(slsem, slval)])
                rec.emit("vector", lambda e: e.tensor_copy(itmp[:], vtmp[:]))
                qd = qtile[t % 2]
                st_prev = sqs[t % 2].n
                rec.emit("vector", lambda e, q=qd: e.tensor_scalar(
                    q[:], itmp[:], -1.0, 1.0, OP.max, OP.min),
                    waits=[(sqs[t % 2], st_prev)], inc=cv)
                q_done[t] = cv.n
                rec.emit("gpsimd", lambda e, q=qd, t=t, d=wq_dst: e.dma_start(
                    out=d[t], in_=q[:]),
                    waits=[(cv, q_done[t])], inc=sqs[t % 2], by=16)
                if t + 4 < NCC:
                    rec.emit("sync", lambda e, t2=t + 4, d=wtile[(t + 4) % 4]:
                             e.dma_start(out=d[:], in_=ws_h[t2]),
                             waits=[(cv, q_done[t])], inc=spr[t % 4], by=16)
                    ld_n[("p2", t + 4)] = (spr[t % 4], spr[t % 4].n)
            return

        ld_n = {}
        prep_weights(w1s_h, w1q_h, m1c, s1pp)
        v_w1done = cv.n
        prep_weights(w2s_h, w2q_h, m2c, s2pp)

        rec.emit("vector", lambda e: e.tensor_scalar(
            cwq[:].rearrange("p a b -> p (a b)"),
            cwq[:].rearrange("p a b -> p (a b)"), m1c[:, 0:1], None, OP.mult),
            waits=[(ld_small, n_small), (cv, v_w1done)])
        rec.emit("vector", lambda e: e.tensor_copy(
            cwb[:].rearrange("p a b -> p (a b)"),
            cwq[:].rearrange("p a b -> p (a b)")))
        rec.emit("vector", lambda e: e.tensor_scalar(
            hs2[:], m2c[:], 0.5 / 127.0, None, OP.mult), inc=cv)

        stx_done = {}
        for k in range(min(4, NTL)):
            rec.emit("gpsimd", lambda e, k=k, d=xt[k % 4]: e.dma_start(
                out=d[:], in_=x_h[k * 128:(k + 1) * 128, :]),
                inc=spx[k % 4], by=16)
            ld_n[("x", k)] = (spx[k % 4], spx[k % 4].n)
        for k in range(NTL):
            slsem, slval = ld_n[("x", k)]
            rec.emit("vector", lambda e, k=k, d=xt[k % 4]: e.tensor_reduce(
                stA_s[:, k:k + 1], d[:], AX.X, OP.add),
                waits=[(slsem, slval)])
            rec.emit("scalar", lambda e, k=k, d=xt[k % 4]: e.activation(
                sq_scr[:], d[:], AF.Square, accum_out=stA_q[:, k:k + 1]),
                waits=[(slsem, slval)], inc=cs)
            rec.emit("vector", lambda e: None, waits=[(cs, cs.n)], inc=cv)
            stx_done[k] = cv.n
            if k + 4 < NTL:
                rec.emit("gpsimd", lambda e, k2=k + 4, d=xt[(k + 4) % 4]:
                         e.dma_start(out=d[:], in_=x_h[k2 * 128:(k2 + 1) * 128, :]),
                         waits=[(cv, stx_done[k])], inc=spx[k % 4], by=16)
                ld_n[("x", k + 4)] = (spx[k % 4], spx[k % 4].n)

        rstd_batch(rec, stA_s, stA_q, mA, rA, mrA)
        rec.emit("sync", None, waits=[(ld_small, n_small)])
        for ssem in sqs:
            rec.emit("sync", None, waits=[(ssem, ssem.n)])
        phase_barrier(rec)
        with nc.Block("prep") as blk:
            rec.replay(blk)

    def ffn_phase(src_h, dst_h, m_t, r_t, mr_t, out_sum, out_sq, name):
        pfx = name + "_"
        rec = Rec(chain={"vector": cv, "scalar": cs, "gpsimd": cg})
        with contextlib.ExitStack() as ps:
            w1r = ps.enter_context(nc.sbuf_tensor(pfx + "w1r", [128, 8, 8, 128], BF16))
            w2r = ps.enter_context(nc.sbuf_tensor(pfx + "w2r", [128, 8, 1024], BF16))
            xln = [ps.enter_context(nc.sbuf_tensor(pfx + f"xln{i}", [128, 1024], F32))
                   for i in range(3)]
            xres = [ps.enter_context(nc.sbuf_tensor(pfx + f"xres{i}", [128, 1024], F32))
                    for i in range(2)]
            m1tm = [ps.enter_context(nc.sbuf_tensor(pfx + f"m1tm{i}", [128, 1024], BF16))
                    for i in range(2)]
            m1f = ps.enter_context(nc.sbuf_tensor(pfx + "m1f", [128, 8, 1024], BF16))
            hb = [ps.enter_context(nc.sbuf_tensor(pfx + f"hb{i}", [128, NCC, 514], BF16))
                  for i in range(2)]
            zb = ps.enter_context(nc.sbuf_tensor(pfx + "zb", [128, NCC, 256], BF16))
            cvt = ps.enter_context(nc.sbuf_tensor(pfx + "cvt", [128, 256], BF16))
            nbf = ps.enter_context(nc.sbuf_tensor(pfx + "nb", [128, NCC, 256], BF16))
            vqb = ps.enter_context(nc.sbuf_tensor(pfx + "vqb", [128, 8, 256], F32))
            niqb = ps.enter_context(nc.sbuf_tensor(pfx + "niqb", [128, 4, 256], I32))
            ystg = [ps.enter_context(nc.sbuf_tensor(pfx + f"ystg{i}", [128, 1024], F32))
                    for i in range(2)]
            s2b = ps.enter_context(nc.sbuf_tensor(pfx + "s2b", [128, 256], F32))
            srow = ps.enter_context(nc.sbuf_tensor(pfx + "srow", [1, 256], F32))
            tq = ps.enter_context(nc.sbuf_tensor(pfx + "tq", [128, 8], F32))
            zc2t = [ps.enter_context(nc.sbuf_tensor(pfx + f"zc2{i}", [128, 1], F32))
                    for i in range(2)]
            rc2t = [ps.enter_context(nc.sbuf_tensor(pfx + f"rc2{i}", [128, 1], F32))
                    for i in range(2)]
            evsc = [[ps.enter_context(nc.sbuf_tensor(pfx + f"evsc{w}_{i}", [128, 1], F32))
                     for i in range(2)] for w in range(2)]
            xnb = ps.enter_context(nc.sbuf_tensor(pfx + "xnb", [128, 1024], F32))
            vxb = xnb
            nxb = ps.enter_context(nc.sbuf_tensor(pfx + "nxb", [128, 512], I32))
            pw1 = [ps.enter_context(nc.psum_tensor(pfx + f"pw1{i}", [128, 512], F32))
                   for i in range(2)]
            pw2 = [ps.enter_context(nc.psum_tensor(pfx + f"pw2{i}", [128, 512], F32))
                   for i in range(4)]
            ptr = ps.enter_context(nc.psum_tensor(pfx + "ptr", [128, 1024], BF16))
            paux = ps.enter_context(nc.psum_tensor(pfx + "paux", [128, 512], F32))

            xl_n = {}
            xr_n = {}
            yst_n = {}
            v_m1e = {}
            hev_s = {}
            halo_l = {}
            halo_r = {}
            p_m1t = {}
            p_w1done = {}
            v_yev = {}
            w1_arr = {}
            w2_arr = {}
            w1_slot_free = {s: 0 for s in range(8)}
            ptr_free = [(cv, 0)]
            paux_free = [0]
            pw1_free = {0: 0, 1: 0, 2: 0, 3: 0}
            pw2_free = [0]
            vq_last = [0]
            v_trees_done = [0]
            yst_prev = [0]
            ystats_n = {}
            pend_stats = []
            NW1B = NB * 8
            NW2B = NB * 2 * 8

            def issue_w1(g, gate):
                if g >= NW1B:
                    return
                bi, bidx = g // 8, g % 8
                cc0 = 4 * bidx
                s0 = (4 * bidx) % 8
                sm = sw1[g % 2]
                rec.emit("sync", lambda e, cc0=cc0, s0=s0: e.dma_start(
                    out=w1r[:, s0:s0 + 4, :, :].rearrange("p c a b -> p c (a b)"),
                    in_=w1q_h[cc0:cc0 + 4].rearrange("c p n -> p c n")),
                    waits=[(cp, gate)], inc=sm, by=16)
                w1_arr[(bi, bidx)] = (sm, sm.n)

            def issue_w2(g, gate):
                if g >= NW2B:
                    return
                wi, bidx = g // 8, g % 8
                cc0 = 4 * bidx
                s0 = (4 * bidx) % 8
                sm = sw2[g % 2]
                rec.emit("sync", lambda e, cc0=cc0, s0=s0: e.dma_start(
                    out=w2r[:, s0:s0 + 4, :],
                    in_=w2q_h[cc0:cc0 + 4].rearrange("c p n -> p c n")),
                    waits=[(cp, gate)], inc=sm, by=16)
                w2_arr[(wi, bidx)] = (sm, sm.n)

            def emit_lnq(b, js=(0, 1, 2, 3)):
                for j in js:
                    k = 4 * b + j
                    slsem, slval = xl_n[k]
                    rec.emit("scalar",
                             lambda e, k=k, d=xln[k % 3]: e.activation(
                                 xnb[:], d[:], AF.Identity,
                                 bias=mr_t[:, k:k + 1], scale=r_t[:, k:k + 1]),
                             waits=[(slsem, slval)])
                    s_ln = cs.n
                    rec.emit("vector", lambda e: e.tensor_reduce(
                        tq[:, 0:1], xnb[:], AX.X, OP.max,
                        apply_absolute_value=True), waits=[(cs, s_ln)])
                    if k + 3 < NTL:
                        gate_v = cv.n
                        rec.emit("sync",
                                 lambda e, k2=k + 3, d=xln[(k + 3) % 3]:
                                 e.dma_start(out=d[:],
                                             in_=src_h[k2 * 128:(k2 + 1) * 128, :]),
                                 waits=[(cv, gate_v)], inc=sxl[k % 3], by=16)
                        xl_n[k + 3] = (sxl[k % 3], sxl[k % 3].n)
                    rec.emit("vector", lambda e: e.tensor_single_scalar(
                        tq[:, 1:2], tq[:, 0:1], EPS, OP.max))
                    rec.emit("vector", lambda e: e.reciprocal(tq[:, 2:3], tq[:, 1:2]))
                    rec.emit("vector", lambda e: e.tensor_scalar(
                        vxb[:], xnb[:], tq[:, 2:3], 127.0, OP.mult, OP.mult))
                    v_vq = cv.n
                    wm = [(cp, p_m1t[k - 2])] if (k - 2) in p_m1t else []
                    for hx in range(2):
                        rec.emit("vector", lambda e, hx=hx: e.tensor_copy(
                            nxb[:], vxb[:, hx * 512:(hx + 1) * 512]))
                        rec.emit("vector",
                                 lambda e, hx=hx, d=m1tm[k % 2]: e.tensor_scalar(
                                     d[:, hx * 512:(hx + 1) * 512], nxb[:],
                                     tq[:, 1:2], 1.0 / 127.0, OP.mult, OP.mult),
                                 waits=(wm if hx == 0 else ()))
                    v_lnq = cv.n
                    for dc in range(NDC):
                        w = [(cv, v_lnq), ptr_free[0]] if dc == 0 else []
                        rec.emit("tensor",
                                 lambda e, dc=dc, sx=m1tm[k % 2]: e.transpose(
                                     ptr[:, dc * 128:(dc + 1) * 128],
                                     sx[:, dc * 128:(dc + 1) * 128], identb[:]),
                                 waits=w, inc=(cp if dc == NDC - 1 else None))
                    p_m1t[k] = cp.n
                    c0 = 128 * (k % 8)
                    wm = [(cp, p_w1done[b - 2])] if (b - 2) in p_w1done else []
                    rec.emit("scalar",
                             lambda e, c0=c0: e.copy(
                                 m1f[:, :, c0:c0 + 128],
                                 ptr[:].rearrange("p (a b) -> p a b", b=128)),
                             waits=[(cp, p_m1t[k])] + wm)
                    v_m1e[k] = (cs.h, cs.n)
                    ptr_free[0] = (cs, cs.n)

            pool_cc_n = {}

            def emit_conv_cc_pool(cc, hbuf, base, first_waits,
                                  defer_silu=False):
                wsl = [cwb[:, cc, j:j + 1].unsqueeze(1).broadcast_to([128, 256, 1])
                       for j in range(3)]
                rec.emit("gpsimd",
                         lambda e, cc=cc, hbuf=hbuf, base=base, w=wsl[0]:
                         e.tensor_tensor(
                             zb[:, cc, :].unsqueeze(2),
                             hbuf[:, cc, base - 1:base + 255].unsqueeze(2),
                             w, OP.mult), waits=first_waits)
                rec.emit("gpsimd",
                         lambda e, cc=cc, hbuf=hbuf, base=base, w=wsl[1]:
                         e.tensor_tensor(
                             cvt[:].unsqueeze(2),
                             hbuf[:, cc, base:base + 256].unsqueeze(2),
                             w, OP.mult))
                rec.emit("gpsimd", lambda e, cc=cc: e.tensor_tensor(
                    zb[:, cc, :], zb[:, cc, :], cvt[:], OP.add))
                rec.emit("gpsimd",
                         lambda e, cc=cc, hbuf=hbuf, base=base, w=wsl[2]:
                         e.tensor_tensor(
                             cvt[:].unsqueeze(2),
                             hbuf[:, cc, base + 1:base + 257].unsqueeze(2),
                             w, OP.mult))
                rec.emit("gpsimd", lambda e, cc=cc: e.tensor_tensor(
                    zb[:, cc, :], zb[:, cc, :], cvt[:], OP.add))
                pool_cc_n[cc] = cg.n
                if not defer_silu:
                    rec.emit("scalar",
                             lambda e, cc=cc: e.activation(
                                 zb[:, cc, :], zb[:, cc, :], AF.Silu,
                                 bias=cbv[:, cc:cc + 1]),
                             waits=[(cg, cg.n)])

            w1banks = [pw1[0], pw1[1], pw2[0], pw2[1]]

            def emit_w1_mm(b, cc, bank_i, extra_w=()):
                mcol = 512 * (b % 2)
                slot = cc % 8
                bank = w1banks[bank_i]
                wsem, wval = w1_arr[(b, cc // 4)]
                mec, men = v_m1e[4 * b + 3]
                w0 = [(wsem, wval), (cs, men),
                      (cs, pw1_free[bank_i])] + list(extra_w)
                for dc in range(NDC):
                    rec.emit("tensor",
                             lambda e, cc=cc, dc=dc, slot=slot, mcol=mcol,
                             bank=bank:
                             e.matmul(bank[:],
                                      w1r[:, slot, dc, :],
                                      m1f[:, dc, mcol:mcol + 512],
                                      start=(dc == 0),
                                      stop=(dc == NDC - 1)),
                             waits=(w0 if dc == 0 else ()),
                             inc=(cp if dc == NDC - 1 else None))
                pcc = cp.n
                w1_slot_free[slot] = pcc
                if cc % 4 == 3:
                    issue_w1(b * 8 + cc // 4 + 2, pcc)
                rec.emit("scalar",
                         lambda e, cc=cc, hb_=hb[b % 2], bank=bank: e.copy(
                             hb_[:, cc, 1:513], bank[:]),
                         waits=[(cp, pcc)])
                pw1_free[bank_i] = cs.n

            def emit_w1_convA(b, bw, yevac_cb):
                hbuf = hb[bw % 2] if bw >= 0 else None
                base = 1
                if bw >= 0:
                    for cc in range(NCONV_A, NCC):
                        fw = []
                        if cc == NCONV_A:
                            fw = [(cs, hev_s.get(bw, 0)),
                                  (cv, halo_l.get(bw, 0)),
                                  (cv, v_trees_done[0])]
                        emit_conv_cc_pool(cc, hbuf, base, fw, defer_silu=True)
                for cc in range(NCC):
                    if bw >= 0 and cc < NCC - NCONV_A:
                        pcc_ = NCONV_A + cc
                        rec.emit("scalar",
                                 lambda e, cc=pcc_: e.activation(
                                     zb[:, cc, :], zb[:, cc, :], AF.Silu,
                                     bias=cbv[:, cc:cc + 1]),
                                 waits=[(cg, pool_cc_n[pcc_])])
                    if b < NB and cc < 16:
                        emit_w1_mm(b, cc, cc % 2)
                    if cc == 16 and yevac_cb is not None:
                        yevac_cb()
                    if bw >= 0:
                        if cc >= NCONV_A:
                            continue
                        wc = []
                        if cc == 0:
                            wc = [(cs, hev_s.get(bw, 0)), (cg, vq_last[0])]
                        rec.emit("vector",
                                 lambda e, cc=cc, hbuf=hbuf: e.tensor_scalar(
                                     zb[:, cc, :], hbuf[:, cc, base - 1:base + 255],
                                     cwq[:, cc, 0:1], None, OP.mult), waits=wc)
                        rec.emit("vector",
                                 lambda e, cc=cc, hbuf=hbuf: e.scalar_tensor_tensor(
                                     zb[:, cc, :], hbuf[:, cc, base:base + 256],
                                     cwq[:, cc, 1:2], zb[:, cc, :], OP.mult, OP.add))
                        rec.emit("vector",
                                 lambda e, cc=cc, hbuf=hbuf: e.scalar_tensor_tensor(
                                     zb[:, cc, :], hbuf[:, cc, base + 1:base + 257],
                                     cwq[:, cc, 2:3], zb[:, cc, :], OP.mult, OP.add))
                        rec.emit("scalar",
                                 lambda e, cc=cc: e.activation(
                                     zb[:, cc, :], zb[:, cc, :], AF.Silu,
                                     bias=cbv[:, cc:cc + 1]),
                                 waits=[(cv, cv.n)])
                return cs.n

            def emit_w1_convB(b):
                if b >= NB:
                    return
                for cc in range(16, NCC):
                    extra = [(cv, yst_prev[0])] if cc in (18, 19) and \
                        yst_prev[0] > 0 else ()
                    emit_w1_mm(b, cc, cc % 4, extra)
                hev_s[b] = cs.n
                p_w1done[b] = cp.n

            def emit_halos(b):
                if b > NB:
                    return
                if b < NB:
                    if b == 0:
                        rec.emit("vector", lambda e: e.memset(hb[0][:, :, 0:1], 0.0))
                    else:
                        rec.emit("vector",
                                 lambda e, dst=hb[b % 2], src=hb[(b - 1) % 2]:
                                 e.tensor_copy(dst[:, :, 0:1], src[:, :, 512:513]),
                                 waits=[(cs, hev_s[b])])
                    halo_l[b] = cv.n
                if b >= 1:
                    if b < NB:
                        rec.emit("vector",
                                 lambda e, dst=hb[(b - 1) % 2], src=hb[b % 2]:
                                 e.tensor_copy(dst[:, :, 513:514], src[:, :, 1:2]),
                                 waits=[(cs, hev_s[b])])
                    else:
                        rec.emit("vector", lambda e: e.memset(
                            hb[(NB - 1) % 2][:, :, 513:514], 0.0))
                    halo_r[b - 1] = cv.n

            def emit_conv_silu(bw, wdx):
                hbuf = hb[bw % 2]
                base = 1 + 256 * wdx
                for cc in range(NCONV_B, NCC):
                    fw = []
                    if cc == NCONV_B:
                        fw = [(cs, hev_s.get(bw, 0)),
                              (cv, halo_r.get(bw, 0))]
                    emit_conv_cc_pool(cc, hbuf, base, fw, defer_silu=True)
                for cc in range(NCC):
                    if cc < NCC - NCONV_B:
                        pcc_ = NCONV_B + cc
                        rec.emit("scalar",
                                 lambda e, cc=pcc_: e.activation(
                                     zb[:, cc, :], zb[:, cc, :], AF.Silu,
                                     bias=cbv[:, cc:cc + 1]),
                                 waits=[(cg, pool_cc_n[pcc_])])
                    if cc >= NCONV_B:
                        continue
                    w0 = []
                    if cc == 0:
                        w0 = [(cs, hev_s.get(bw, 0)), (cg, vq_last[0])]
                    rec.emit("vector",
                             lambda e, cc=cc, base=base, hbuf=hbuf:
                             e.tensor_scalar(
                                 zb[:, cc, :], hbuf[:, cc, base - 1:base + 255],
                                 cwq[:, cc, 0:1], None, OP.mult), waits=w0)
                    rec.emit("vector",
                             lambda e, cc=cc, base=base, hbuf=hbuf:
                             e.scalar_tensor_tensor(
                                 zb[:, cc, :], hbuf[:, cc, base:base + 256],
                                 cwq[:, cc, 1:2], zb[:, cc, :], OP.mult, OP.add))
                    rec.emit("vector",
                             lambda e, cc=cc, base=base, hbuf=hbuf:
                             e.scalar_tensor_tensor(
                                 zb[:, cc, :], hbuf[:, cc, base + 1:base + 257],
                                 cwq[:, cc, 2:3], zb[:, cc, :], OP.mult, OP.add))
                    rec.emit("scalar",
                             lambda e, cc=cc: e.activation(
                                 zb[:, cc, :], zb[:, cc, :], AF.Silu,
                                 bias=cbv[:, cc:cc + 1]),
                             waits=[(cv, cv.n)])
                return cs.n

            def emit_trees_scale_quant(bw, wdx, s_silu, tile_cb=None):
                tree = [(0, 16, None, OP.max), (16, 24, (0, 16), OP.max),
                        (24, 28, (16, 24), OP.max), (28, 30, (24, 28), OP.max),
                        (30, 31, (28, 30), OP.max),
                        (0, 16, None, OP.min), (16, 24, (0, 16), OP.min),
                        (24, 28, (16, 24), OP.min), (28, 30, (24, 28), OP.min),
                        (31, 32, (28, 30), OP.min)]
                first = True
                for d0, d1, srcr, op in tree:
                    if srcr is None:
                        fn = (lambda e, d0=d0, d1=d1, op=op: e.tensor_tensor(
                            nbf[:, d0:d1, :], zb[:, 0::2, :], zb[:, 1::2, :], op))
                    else:
                        s0, s1 = srcr
                        fn = (lambda e, d0=d0, d1=d1, s0=s0, s1=s1, op=op:
                              e.tensor_tensor(nbf[:, d0:d1, :], nbf[:, s0:s1:2, :],
                                              nbf[:, s0 + 1:s1:2, :], op))
                    w = [(cs, s_silu), (cp, pw2_free[0])] if first else []
                    first = False
                    rec.emit("vector", fn, waits=w)
                v_tree = cv.n
                v_trees_done[0] = v_tree
                trsp = [(30, 0, 0), (30, 128, 128), (31, 0, 256), (31, 128, 384)]
                for i, (scc, soff, doff) in enumerate(trsp):
                    w = [(cv, v_tree), ptr_free[0]] if i == 0 else []
                    rec.emit("tensor",
                             lambda e, scc=scc, soff=soff, doff=doff:
                             e.transpose(ptr[:, doff:doff + 128],
                                         nbf[:, scc, soff:soff + 128], identb[:]),
                             waits=w, inc=(cp if i == 3 else None))
                p_trw = cp.n
                for ti in range(2):
                    rec.emit("vector", lambda e, ti=ti: e.tensor_reduce(
                        tq[:, 3:4], ptr[:, ti * 128:(ti + 1) * 128], AX.X, OP.max),
                        waits=([(cp, p_trw)] if ti == 0 else ()))
                    rec.emit("vector", lambda e, ti=ti: e.tensor_reduce(
                        tq[:, 4:5], ptr[:, 256 + ti * 128:384 + ti * 128],
                        AX.X, OP.min))
                    rec.emit("vector", lambda e: e.tensor_single_scalar(
                        tq[:, 4:5], tq[:, 4:5], -1.0, OP.mult))
                    rec.emit("vector", lambda e: e.tensor_tensor(
                        tq[:, 3:4], tq[:, 3:4], tq[:, 4:5], OP.max))
                    rec.emit("vector", lambda e, d=zc2t[ti]: e.tensor_single_scalar(
                        d[:], tq[:, 3:4], EPS, OP.max))
                    rec.emit("vector", lambda e, d=evsc[wdx][ti], z=zc2t[ti]:
                             e.tensor_scalar(d[:], z[:], hs2[:, 0:1], None, OP.mult))
                    rec.emit("vector", lambda e, d=rc2t[ti], z=zc2t[ti]:
                             e.reciprocal(d[:], z[:]))
                v_zc = cv.n
                ptr_free[0] = (cv, cv.n)
                for ti in range(2):
                    w = [(cv, v_zc), (cv, paux_free[0])] if ti == 0 else []
                    rec.emit("tensor", lambda e, ti=ti, z=rc2t[ti]: e.transpose(
                        paux[0:1, ti * 128:(ti + 1) * 128], z[:], identf[:]),
                        waits=w, inc=(cp if ti == 1 else None))
                p_srt = cp.n
                rec.emit("vector", lambda e: e.tensor_copy(srow[:], paux[0:1, 0:256]),
                         waits=[(cp, p_srt)])
                v_srow = cv.n
                rec.emit("tensor", lambda e: e.matmul(
                    paux[:, 256:512], ones_r[:], srow[:], start=True, stop=True),
                    waits=[(cv, v_srow)], inc=cp)
                p_bc = cp.n
                rec.emit("vector", lambda e: e.tensor_scalar(
                    s2b[:], paux[:, 256:512], 127.0, None, OP.mult),
                    waits=[(cp, p_bc)])
                v_s2b = cv.n
                paux_free[0] = cv.n
                i32_done = {}
                bf_done = {}
                vq_done = {}
                for g in range(8):
                    hfq = (g % 2) * 4
                    wq_ = [(cv, v_s2b)]
                    if g - 2 in i32_done:
                        wq_.append((cv, i32_done[g - 2]))
                    rec.emit("gpsimd", lambda e, g=g, hfq=hfq: e.tensor_tensor(
                        vqb[:, hfq:hfq + 4, :], zb[:, 4 * g:4 * g + 4, :],
                        s2b[:].unsqueeze(1).broadcast_to([128, 4, 256]),
                        OP.mult), waits=wq_)
                    vq_done[g] = cg.n
                    rec.emit("vector", lambda e, hfq=hfq: e.tensor_copy(
                        niqb[:], vqb[:, hfq:hfq + 4, :]),
                        waits=[(cg, vq_done[g])])
                    i32_done[g] = cv.n
                    rec.emit("vector", lambda e, g=g: e.tensor_copy(
                        nbf[:, 4 * g:4 * g + 4, :], niqb[:]))
                    bf_done[g] = cv.n
                    if tile_cb is not None and g % 2 == 1:
                        tile_cb()
                vq_last[0] = vq_done[7]
                return bf_done

            def emit_w2(bw, wdx, bf_done):
                wgl = 2 * bw + wdx
                for cc in range(NCC):
                    w2a = w2_arr[(wgl, cc // 4)]
                    for mi, (ti, hf) in enumerate([(0, 0), (0, 1), (1, 0), (1, 1)]):
                        w = []
                        if cc % 4 == 0 and mi == 0:
                            w = [(cv, bf_done[cc // 4])]
                        if cc == 0 and mi == 0:
                            if yst_prev[0] > 0:
                                w.append((cv, yst_prev[0]))
                            w.append((cs, max(pw1_free.values())))
                        if mi == 0:
                            w.append(w2a)
                        rec.emit("tensor",
                                 lambda e, cc=cc, ti=ti, hf=hf: e.matmul(
                                     pw2[2 * ti + hf][:],
                                     nbf[:, cc, ti * 128:(ti + 1) * 128],
                                     w2r[:, cc % 8, hf * 512:(hf + 1) * 512],
                                     start=(cc == 0), stop=(cc == NCC - 1)),
                                 waits=w,
                                 inc=(cp if (cc % 4 == 3 and mi == 3) else None))
                    if cc % 4 == 3:
                        issue_w2(wgl * 8 + cc // 4 + 2, cp.n)
                p_w2 = cp.n
                pw2_free[0] = p_w2
                return p_w2

            def emit_yevac(bw, wdx, p_w2):
                tok0 = 512 * bw + 256 * wdx
                for ti in range(2):
                    kk = tok0 // 128 + ti
                    xr_sem, xr_val = xr_n[kk]
                    ysl = kk % 2
                    w = [(cp, p_w2), (xr_sem, xr_val)]
                    stg = yst_n.get(kk - 2)
                    if stg is not None:
                        w.append(stg)
                    if kk - 2 in ystats_n:
                        w.append((cs, ystats_n[kk - 2]))
                    rec.emit("vector",
                             lambda e, ti=ti, ysl=ysl, kk=kk: e.scalar_tensor_tensor(
                                 ystg[ysl][:, 0:512], pw2[2 * ti + 0][:],
                                 evsc[wdx][ti][:, 0:1], xres[kk % 2][:, 0:512],
                                 OP.mult, OP.add), waits=w)
                    rec.emit("vector",
                             lambda e, ti=ti, ysl=ysl, kk=kk: e.scalar_tensor_tensor(
                                 ystg[ysl][:, 512:1024], pw2[2 * ti + 1][:],
                                 evsc[wdx][ti][:, 0:1], xres[kk % 2][:, 512:1024],
                                 OP.mult, OP.add))
                    v_ev = cv.n
                    v_yev[kk] = v_ev
                    wdma = [(cv, v_ev)]
                    if out_sum is not None:
                        def _stats(ysl=ysl, kk=kk, v_ev=v_ev):
                            rec.emit("scalar",
                                     lambda e: e.activation(
                                         sq_scr[:], ystg[ysl][:], AF.Identity,
                                         accum_out=out_sum[:, kk:kk + 1]),
                                     waits=[(cv, v_ev)])
                            rec.emit("scalar",
                                     lambda e: e.activation(
                                         sq_scr[:], ystg[ysl][:], AF.Square,
                                         accum_out=out_sq[:, kk:kk + 1]))
                            ystats_n[kk] = cs.n
                        pend_stats.append(_stats)
                    ssem = sst[ysl]
                    rec.emit("sync", lambda e, ysl=ysl, kk=kk: e.dma_start(
                        out=dst_h[kk * 128:(kk + 1) * 128, :], in_=ystg[ysl][:]),
                        waits=wdma, inc=ssem, by=16)
                    yst_n[kk] = (ssem, ssem.n)
                    if kk + 2 < NTL:
                        rec.emit("sync",
                                 lambda e, k2=kk + 2, d=xres[(kk + 2) % 2]:
                                 e.dma_start(out=d[:],
                                             in_=src_h[k2 * 128:(k2 + 1) * 128, :]),
                                 waits=[(cv, v_ev)], inc=sxr[(kk + 2) % 2], by=16)
                        xr_n[kk + 2] = (sxr[(kk + 2) % 2], sxr[(kk + 2) % 2].n)
                yst_prev[0] = cv.n

            for k in range(3):
                rec.emit("sync", lambda e, k=k, d=xln[k % 3]: e.dma_start(
                    out=d[:], in_=src_h[k * 128:(k + 1) * 128, :]),
                    inc=sxl[k % 3], by=16)
                xl_n[k] = (sxl[k % 3], sxl[k % 3].n)
            for kk in range(2):
                rec.emit("sync", lambda e, kk=kk, d=xres[kk % 2]: e.dma_start(
                    out=d[:], in_=src_h[kk * 128:(kk + 1) * 128, :]),
                    inc=sxr[kk % 2], by=16)
                xr_n[kk] = (sxr[kk % 2], sxr[kk % 2].n)
            issue_w1(0, 0)
            issue_w1(1, 0)
            issue_w2(0, 0)
            issue_w2(1, 0)

            emit_lnq(0)
            pend_w1 = [None]
            for b in range(NB + 2):
                bw = b - 1
                ycb = None
                if pend_w1[0] is not None:
                    pbw, pbf = pend_w1[0]
                    pw = emit_w2(pbw, 1, pbf)
                    ycb = (lambda pbw=pbw, pw=pw: emit_yevac(pbw, 1, pw))
                    pend_w1[0] = None
                if bw < NB:
                    s_silu0 = emit_w1_convA(b, bw, ycb)
                else:
                    s_silu0 = None
                    if ycb is not None:
                        ycb()
                pw0 = None
                ltiles = list(range(4)) if b + 1 < NB else []
                lt_i = [0]

                def tile_cb():
                    if lt_i[0] < len(ltiles):
                        emit_lnq(b + 1, (ltiles[lt_i[0]],))
                        lt_i[0] += 1
                if 0 <= bw < NB:
                    emit_w1_convB(b)
                    bf0 = emit_trees_scale_quant(bw, 0, s_silu0, tile_cb)
                    pw0 = emit_w2(bw, 0, bf0)
                else:
                    emit_w1_convB(b)
                for st_cb in pend_stats:
                    st_cb()
                del pend_stats[:]
                emit_halos(b)
                if 0 <= bw < NB:
                    s_silu1 = emit_conv_silu(bw, 1)
                    emit_yevac(bw, 0, pw0)
                    bf1 = emit_trees_scale_quant(bw, 1, s_silu1)
                    while lt_i[0] < len(ltiles):
                        tile_cb()
                    pend_w1[0] = (bw, bf1)
                elif b + 1 < NB:
                    emit_lnq(b + 1)
                for st_cb in pend_stats:
                    st_cb()
                del pend_stats[:]

            for ssem in sst:
                rec.emit("sync", None, waits=[(ssem, ssem.n)])
            if out_sum is not None:
                rstd_next[0](rec)
            phase_barrier(rec)
            with nc.Block(name) as blk:
                rec.replay(blk)


    rstd_next = [None]
    rstd_next[0] = lambda rec: rstd_batch(rec, stB_s, stB_q, mB, rB)
    ffn_phase(x_h, y1_h, mA, rA, mrA, stB_s, stB_q, "passA")

    rec = Rec(chain={"vector": cv, "scalar": cs, "gpsimd": cg})
    with contextlib.ExitStack() as ps:
        wq = [ps.enter_context(nc.sbuf_tensor(f"b_wq{i}", [128, 3072], BF16))
              for i in range(NDC)]
        Ssb = [ps.enter_context(nc.sbuf_tensor(f"b_S{i}", [128, 1024], BF16))
               for i in range(NDC)]
        ylr = [ps.enter_context(nc.sbuf_tensor(f"b_yl{i}", [128, 1024], F32))
               for i in range(8)]
        xn2 = [ps.enter_context(nc.sbuf_tensor(f"b_xn2{i}", [128, 1024], BF16))
               for i in range(2)]
        xn2f = [ps.enter_context(nc.sbuf_tensor(f"b_xn2f{i}", [128, 8, 512],
                                                 BF16)) for i in range(2)]
        qf = [ps.enter_context(nc.sbuf_tensor(f"b_qf{i}", [128, 8, 512], BF16))
              for i in range(2)]
        kf = [ps.enter_context(nc.sbuf_tensor(f"b_kf{i}", [128, 8, 512], BF16))
              for i in range(2)]
        vf = ps.enter_context(nc.sbuf_tensor("b_vf", [128, 8, 512], BF16))
        vtm = [ps.enter_context(nc.sbuf_tensor(f"b_vtm{i}", [128, 1024], BF16))
               for i in range(4)]
        ktm = [ps.enter_context(nc.sbuf_tensor(f"b_ktm{i}", [128, 1024], BF16))
               for i in range(4)]
        sT = [ps.enter_context(nc.sbuf_tensor(f"b_sT{i}", [128, 512], BF16))
              for i in range(4)]
        y2st = [ps.enter_context(nc.sbuf_tensor(f"b_y2st{i}", [128, 1024], F32))
                for i in range(2)]
        pq = [ps.enter_context(nc.psum_tensor(f"b_pq{i}", [128, 512], F32))
              for i in range(2)]
        psc = ps.enter_context(nc.psum_tensor("b_psc", [128, 512], F32))
        pro = [ps.enter_context(nc.psum_tensor(f"b_pro{i}", [128, 512], F32))
               for i in range(2)]
        pst = [ps.enter_context(nc.psum_tensor(f"b_pst{i}", [128, 512], F32))
               for i in range(2)]
        ptrB = ps.enter_context(nc.psum_tensor("b_ptrB", [128, 1024], BF16))
        ptrB2 = ptrB

        for dc in range(NDC):
            rec.emit("gpsimd", lambda e, dc=dc: e.dma_start(
                out=wq[dc][:], in_=wqkvT_h[dc * 128:(dc + 1) * 128, :]),
                inc=swq, by=16)
        n_wq = swq.n
        for dc in range(NDC):
            rec.emit("vector", lambda e, dc=dc: e.memset(Ssb[dc][:], 0.0),
                     inc=(cv if dc == NDC - 1 else None))
        v_supd = cv.n

        yl_n = {}
        y2st_n = {}
        st_upd_v = {}
        v_y2ev = {}
        s_qkv_ev = {}
        p_tr_tile = {}
        p_qkv_done = {}
        pst_free = {0: 0, 1: 0}
        ptrB_free = [0]
        ptrB2_free = [0]
        for k in range(8):
            rec.emit("sync", lambda e, k=k, d=ylr[k % 8]: e.dma_start(
                out=d[:], in_=y1_h[k * 128:(k + 1) * 128, :]),
                inc=syl[k % 8], by=16)
            yl_n[k] = (syl[k % 8], syl[k % 8].n)

        v_xn2f = {}
        s_qkv_done = {}

        def emit_ln2(tc):
            for j in range(4):
                k = 4 * tc + j
                pb = ptrB
                pbf = ptrB_free
                slsem, slval = yl_n[k]
                wm = [(slsem, slval)]
                if k - 2 in p_tr_tile:
                    wm.append((cp, p_tr_tile[k - 2]))
                rec.emit("vector", lambda e, k=k, j=j, d=ylr[k % 8]:
                         e.tensor_scalar(xn2[j % 2][:], d[:], mB[:, k:k + 1],
                                         rB[:, k:k + 1], OP.subtract, OP.mult),
                         waits=wm, inc=cv)
                v_ln2 = cv.n
                for dc in range(NDC):
                    w = []
                    if dc == 0:
                        w = [(cv, v_ln2), (cv, pbf[0])]
                    rec.emit("tensor", lambda e, dc=dc, sx=xn2[j % 2], pb=pb:
                             e.transpose(pb[:, dc * 128:(dc + 1) * 128],
                                         sx[:, dc * 128:(dc + 1) * 128],
                                         identb[:]),
                             waits=w, inc=(cp if dc == NDC - 1 else None))
                p_tr_tile[k] = cp.n
                wm = []
                if tc - 2 in p_qkv_done:
                    wm = [(cp, p_qkv_done[tc - 2])]
                rec.emit("vector", lambda e, j=j, pb=pb, x2=xn2f[tc % 2]:
                         e.tensor_copy(
                             x2[:, :, j * 128:(j + 1) * 128],
                             pb[:].rearrange("p (a b) -> p a b", b=128)),
                         waits=[(cp, p_tr_tile[k])] + wm, inc=cv)
                pbf[0] = cv.n
            v_xn2f[tc] = cv.n

        def emit_qkv_group(tc, oc):
            for dc in range(NDC):
                w = []
                if dc == 0:
                    w = [(cv, v_xn2f[tc]), (swq, n_wq)]
                    if oc >= 2:
                        w.append((cs, s_qkv_ev[(tc, oc - 2)]))
                    elif (tc - 1, 22 + oc) in s_qkv_ev:
                        w.append((cs, s_qkv_ev[(tc - 1, 22 + oc)]))
                rec.emit("tensor", lambda e, tc=tc, oc=oc, dc=dc: e.matmul(
                    pq[oc % 2][:], wq[dc][:, oc * 128:(oc + 1) * 128],
                    xn2f[tc % 2][:, dc, :],
                    start=(dc == 0), stop=(dc == NDC - 1)),
                    waits=w, inc=(cp if dc == NDC - 1 else None))
            p_q = cp.n
            dsts = [qf[tc % 2], kf[tc % 2], vf][oc // 8]
            rec.emit("scalar", lambda e, oc=oc, d=dsts: e.activation(
                d[:, oc % 8, :], pq[oc % 2][:], AF.Identity,
                bias=bqkv[:, oc:oc + 1]),
                waits=[(cp, p_q)], inc=cs)
            s_qkv_ev[(tc, oc)] = cs.n
            if oc == 23:
                p_qkv_done[tc] = cp.n
                s_qkv_done[tc] = cs.n

        emit_ln2(0)
        for oc in range(24):
            emit_qkv_group(0, oc)

        for c in range(NCHK):
            tposes = [(jb, which) for jb in range(4) for which in ("v", "k")]
            for idx, (jb, which) in enumerate(tposes):
                pb = ptrB
                pbf = ptrB_free
                src = vf if which == "v" else kf[c % 2]
                dst = vtm[jb] if which == "v" else ktm[jb]
                for dv in range(NDC):
                    w = []
                    if dv == 0:
                        w = [(cs, s_qkv_done[c]), (cv, pbf[0])]
                    rec.emit("tensor", lambda e, dv=dv, jb=jb, src=src, pb=pb:
                             e.transpose(pb[:, dv * 128:(dv + 1) * 128],
                                         src[:, dv, jb * 128:(jb + 1) * 128],
                                         identb[:]),
                             waits=w, inc=(cp if dv == NDC - 1 else None))
                p_tr = cp.n
                if which == "v":
                    rec.emit("vector", lambda e, dst=dst, pb=pb: e.tensor_copy(
                        dst[:], pb[:]), waits=[(cp, p_tr)], inc=cv)
                else:
                    rec.emit("vector", lambda e, dst=dst, jb=jb, pb=pb:
                             e.tensor_scalar(dst[:], pb[:],
                                             kdec[:, jb:jb + 1], None,
                                             OP.mult),
                             waits=[(cp, p_tr)], inc=cv)
                pbf[0] = cv.n
            v_trans = cv.n
            if c + 1 < NCHK:
                emit_ln2(c + 1)
            v_sc_ev = cv.n
            for jb in range(4):
                ni = 512 - jb * 128
                for dc in range(NDC):
                    w = []
                    if dc == 0:
                        w = [(cv, v_sc_ev), (cv, v_trans)]
                    rec.emit("tensor",
                             lambda e, jb=jb, dc=dc, ni=ni, kk=kf[c % 2],
                             qq=qf[c % 2]: e.matmul(
                                 psc[:, 0:ni], kk[:, dc, jb * 128:(jb + 1) * 128],
                                 qq[:, dc, jb * 128:512],
                                 start=(dc == 0), stop=(dc == NDC - 1)),
                             waits=w, inc=(cp if dc == NDC - 1 else None))
                p_sc = cp.n
                aimm = float(ALPHA ** (-128.0 * jb))
                rec.emit("vector", lambda e, jb=jb, aimm=aimm:
                         e.scalar_tensor_tensor(sT[jb][:, 0:128], psc[:, 0:128],
                                                aimm, trik[:], OP.mult, OP.mult),
                         waits=[(cp, p_sc)])
                if ni > 128:
                    rec.emit("vector", lambda e, jb=jb, ni=ni, aimm=aimm:
                             e.tensor_scalar(sT[jb][:, 128:ni], psc[:, 128:ni],
                                             kcol[:, 0:1], aimm, OP.mult,
                                             OP.mult))
                rec.emit("vector", lambda e: None, inc=cv)
                v_sc_ev = cv.n
            v_scores = cv.n
            for it in range(4):
                k = 4 * c + it
                for hf in range(2):
                    mms = []
                    if c > 0:
                        mms += [("cross", dc) for dc in range(NDC)]
                    mms += [("av", jb) for jb in range(it + 1)]
                    for mi, (kind, idx) in enumerate(mms):
                        w = []
                        if mi == 0:
                            w = [(cv, v_scores), (cv, v_supd)]
                            prev = (v_y2ev.get((c, it - 1, hf)) if it > 0
                                    else v_y2ev.get((c - 1, 3, hf)))
                            if prev is not None:
                                w.append((cv, prev))
                        st = (mi == 0)
                        sp = (mi == len(mms) - 1)
                        if kind == "cross":
                            rec.emit("tensor",
                                     lambda e, it=it, hf=hf, dc=idx, st=st, sp=sp,
                                     qq=qf[c % 2]:
                                     e.matmul(pro[hf][:],
                                              qq[:, dc, it * 128:(it + 1) * 128],
                                              Ssb[dc][:, hf * 512:(hf + 1) * 512],
                                              start=st, stop=sp),
                                     waits=w, inc=(cp if sp else None))
                        else:
                            rec.emit("tensor",
                                     lambda e, it=it, hf=hf, jb=idx, st=st, sp=sp:
                                     e.matmul(
                                         pro[hf][:],
                                         sT[jb][:, (it - jb) * 128:
                                                (it - jb + 1) * 128],
                                         vtm[jb][:, hf * 512:(hf + 1) * 512],
                                         start=st, stop=sp),
                                     waits=w, inc=(cp if sp else None))
                    p_ro = cp.n
                    ysl = k % 2
                    slsem, slval = yl_n[k]
                    w = [(cp, p_ro), (slsem, slval)]
                    stg = y2st_n.get(k - 2)
                    if stg is not None:
                        w.append(stg)
                    rec.emit("vector",
                             lambda e, it=it, hf=hf, ysl=ysl, k=k:
                             e.scalar_tensor_tensor(
                                 y2st[ysl][:, hf * 512:(hf + 1) * 512],
                                 pro[hf][:], evk[:, it:it + 1],
                                 ylr[k % 8][:, hf * 512:(hf + 1) * 512],
                                 OP.mult, OP.add),
                             waits=w, inc=cv)
                    v_y2ev[(c, it, hf)] = cv.n
                    if c + 1 < NCHK:
                        g0 = 3 * (2 * it + hf)
                        for oc in range(g0, g0 + 3):
                            emit_qkv_group(c + 1, oc)
                v_ev = cv.n
                rec.emit("vector", lambda e, ysl=k % 2, k=k: e.tensor_reduce(
                    stC_s[:, k:k + 1], y2st[ysl][:], AX.X, OP.add), inc=cv)
                rec.emit("scalar", lambda e, ysl=k % 2, k=k: e.activation(
                    sq_scr[:], y2st[ysl][:], AF.Square,
                    accum_out=stC_q[:, k:k + 1]),
                    waits=[(cv, cv.n)], inc=cs)
                ssem = sst[k % 2]
                rec.emit("sync", lambda e, ysl=k % 2, k=k: e.dma_start(
                    out=y2_h[k * 128:(k + 1) * 128, :], in_=y2st[ysl][:]),
                    waits=[(cv, cv.n), (cs, cs.n)], inc=ssem, by=16)
                y2st_n[k] = (ssem, ssem.n)
                if k + 8 < NTL:
                    rec.emit("sync", lambda e, k2=k + 8, d=ylr[(k + 8) % 8]:
                             e.dma_start(out=d[:],
                                         in_=y1_h[k2 * 128:(k2 + 1) * 128, :]),
                             waits=[(cv, v_ev)], inc=syl[(k + 8) % 8], by=16)
                    yl_n[k + 8] = (syl[(k + 8) % 8], syl[(k + 8) % 8].n)
            p_retout_all = cp.n
            if c < NCHK - 1:
                for dc in range(NDC):
                    for hf in range(2):
                        bank = hf
                        for jb in range(4):
                            w = []
                            if jb == 0:
                                w = [(cv, v_trans)]
                                if pst_free[bank] > 0:
                                    w.append((cv, pst_free[bank]))
                            rec.emit("tensor",
                                     lambda e, dc=dc, hf=hf, jb=jb, bank=bank:
                                     e.matmul(
                                         pst[bank][:],
                                         ktm[jb][:, dc * 128:(dc + 1) * 128],
                                         vtm[jb][:, hf * 512:(hf + 1) * 512],
                                         start=(jb == 0), stop=(jb == 3)),
                                     waits=w, inc=(cp if jb == 3 else None))
                        p_st = cp.n
                        rec.emit("vector",
                                 lambda e, dc=dc, hf=hf, bank=bank:
                                 e.scalar_tensor_tensor(
                                     Ssb[dc][:, hf * 512:(hf + 1) * 512],
                                     Ssb[dc][:, hf * 512:(hf + 1) * 512], A512,
                                     pst[bank][:], OP.mult, OP.add),
                                 waits=[(cp, p_st), (cp, p_retout_all)], inc=cv)
                        pst_free[bank] = cv.n
                v_supd = cv.n
        for ssem in sst:
            rec.emit("sync", None, waits=[(ssem, ssem.n)])
        rstd_batch(rec, stC_s, stC_q, mC, rC, mrC)
        phase_barrier(rec)
        with nc.Block("passB") as blk:
            rec.replay(blk)

    ffn_phase(y2_h, out_h, mC, rC, mrC, None, None, "passC")

    rec = Rec(chain={"vector": cv, "scalar": cs, "gpsimd": cg})
    for ssem in sst:
        rec.emit("sync", None, waits=[(ssem, ssem.n)])
    with nc.Block("fin") as blk:
        rec.replay(blk)

    es.close()
    return nc


def _lay(a):
    return np.ascontiguousarray(a)


_NC_CACHE = {}


def kernel(L0_attn_x, L0_attn_W_qkv, L0_attn_b_qkv, L0_ffn_w1, L0_ffn_conv_w,
           L0_ffn_conv_b, L0_ffn_w2):
    x = np.asarray(L0_attn_x, dtype=np.float32)
    Wqkv = np.asarray(L0_attn_W_qkv, dtype=np.float32)
    bq = np.asarray(L0_attn_b_qkv, dtype=np.float32)
    w1 = np.asarray(L0_ffn_w1, dtype=np.float32)
    cw = np.asarray(L0_ffn_conv_w, dtype=np.float32)
    cb = np.asarray(L0_ffn_conv_b, dtype=np.float32)
    w2 = np.asarray(L0_ffn_w2, dtype=np.float32)

    w1s = _lay(w1.T.reshape(8, 128, 32, 128).transpose(2, 1, 0, 3)
               .reshape(32, 128, 1024))
    w2s = _lay(w2.T.reshape(32, 128, 1024))
    wqkvT = _lay(Wqkv.T)
    bq_l = _lay(bq.reshape(24, 128).T)
    cw_l = _lay(cw.reshape(32, 128, 3).transpose(1, 0, 2))
    cb_l = _lay(cb.reshape(32, 128).T)

    p = np.arange(128, dtype=np.float64)
    identb = np.eye(128, dtype=bfdt)
    identf = np.eye(128, dtype=np.float32)
    kcol = (ALPHA ** (-(p + 1))).astype(np.float32).reshape(128, 1)
    ii = np.arange(128, dtype=np.float64)
    trik = np.where(ii[None, :] >= p[:, None],
                    (ALPHA ** (-(p + 1)))[:, None], 0.0).astype(np.float32)
    evk = np.stack([(ALPHA ** (k * 128 + p + 1)) / 32.0 for k in range(4)],
                   axis=1).astype(np.float32)
    kdec = np.stack([ALPHA ** (CH - 1 - (k * 128 + p)) for k in range(4)],
                    axis=1).astype(np.float32)

    if "nc" not in _NC_CACHE:
        _NC_CACHE["nc"] = build_program()
    nc = _NC_CACHE["nc"]

    in_maps = [{
        "x": _lay(x[i]), "w1s": w1s, "w2s": w2s, "wqkvT": wqkvT,
        "bqkvL": bq_l, "cwL": cw_l, "cbL": cb_l, "identb": identb,
        "identf": identf, "kcol": kcol, "trik": trik, "evk": evk, "kdec": kdec,
    } for i in range(8)]
    res = run_bass_kernel_spmd(nc, in_maps, core_ids=list(range(8)))
    return np.stack([r["out"] for r in res.results], axis=0)
